# version 02
# baseline (speedup 1.0000x reference)
import numpy as np
import ml_dtypes
import concourse.bass as bass
import concourse.mybir as mybir
from concourse.bass_utils import run_bass_kernel_spmd

F32 = mybir.dt.float32
BF16 = mybir.dt.bfloat16
I32 = mybir.dt.int32
AF = mybir.ActivationFunctionType
ALU = mybir.AluOpType
BF = ml_dtypes.bfloat16

H = 4096
PROJ = 8512
PROJP = 8576
CONV = 4352
INTER = 11008
NH = 64
EPS = 1e-5
TPC = 1024
NCH = 8
Q = 128
NKT = 32
NMI_G = 86
NMI_P = 67
NPT_C = 34

_UPTO = None
_DETECT = True


_CHAIN_OPS = frozenset((
    "tensor_copy", "tensor_tensor", "tensor_scalar", "tensor_single_scalar",
    "tensor_scalar_add", "tensor_scalar_sub", "tensor_scalar_mul",
    "tensor_scalar_max", "tensor_scalar_min", "tensor_relu", "tensor_add",
    "tensor_sub", "tensor_mul", "tensor_max", "scalar_tensor_tensor",
    "tensor_tensor_scan", "tensor_reduce", "reduce_max", "reduce_sum",
    "memset", "memzero", "engine_nop", "select", "transpose", "reciprocal",
    "activation", "square", "sqrt", "mul", "copy", "add", "sign",
))


class _ChainEng:

    def __init__(self, C, eng, semname):
        self._C = C
        self._eng = eng
        self._sem = semname

    def __getattr__(self, name):
        f = getattr(self._eng, name)
        if name not in _CHAIN_OPS:
            return f
        C = self._C

        def wrapped(*a, **k):
            s, v = C.latest(self._sem)
            if v > 0:
                self._eng.wait_ge(C.sem[s], v)
            inst = f(*a, **k)
            inst.then_inc(C.sem[s], 1)
            C.cnt[s] += 1
            return inst

        return wrapped


class Ctx:

    def __init__(self, nc):
        self.nc = nc
        self.sem = {}
        self.cnt = {}
        self.g = nc.gpsimd
        self.w = nc.sync
        self.t = nc.tensor
        self.v = _ChainEng(self, nc.vector, "sV")
        self.s = _ChainEng(self, nc.scalar, "sA")
        self._chain = {"sV": self.v, "sA": self.s}

    def _get(self, s):
        if s not in self.sem:
            self.sem[s] = self.nc.alloc_semaphore(s)
            self.cnt[s] = 0
        return self.sem[s]

    def inc(self, inst, s, n=1):
        if s in ("sV", "sA"):
            return (s, self.cnt[s])
        h = self._get(s)
        inst.then_inc(h, n)
        self.cnt[s] += n
        return (s, self.cnt[s])

    def latest(self, s):
        self._get(s)
        return (s, self.cnt[s])

    def wtok(self, eng, *toks):
        for t in toks:
            if t is None:
                continue
            s, v = t
            if v > 0:
                eng.wait_ge(self.sem[s], v)

    def dma(self, eng, out, in_, s):
        i = eng.dma_start(out=out, in_=in_)
        return self.inc(i, s, 16)



def _pack_lhsT(w_km):
    K, M = w_km.shape
    return np.ascontiguousarray(
        w_km.reshape(K // 128, 128, M // 128, 128).transpose(2, 1, 0, 3))


def _pack_cols(v):
    return np.ascontiguousarray(v.reshape(-1, 128).T)


def _prep_host(inputs):
    f = lambda a: np.ascontiguousarray(np.asarray(a))
    hp = {}
    wg = f(inputs["L0_ffn_w_gate"]).astype(np.int32)
    wu = f(inputs["L0_ffn_w_up"]).astype(np.int32)
    wd = f(inputs["L0_ffn_w_down"]).astype(np.int32)
    hp["Wg"] = _pack_lhsT(wg.T)
    hp["Wu"] = _pack_lhsT(wu.T)
    hp["Wd"] = _pack_lhsT(wd.T)
    wip = f(inputs["L0_attn_in_proj_w"]).astype(np.float32)
    wip_p = np.zeros((H, PROJP), np.float32)
    wip_p[:, :PROJ] = wip
    hp["Wip"] = _pack_lhsT(wip_p)
    hp["Wop"] = _pack_lhsT(f(inputs["L0_attn_out_proj_w"]).astype(np.float32))
    hp["sgate"] = _pack_cols(f(inputs["L0_ffn_s_gate"]).astype(np.float32))
    hp["sup"] = _pack_cols(f(inputs["L0_ffn_s_up"]).astype(np.float32))
    hp["sdown"] = _pack_cols(f(inputs["L0_ffn_s_down"]).astype(np.float32))
    mup_p = np.zeros((PROJP,), np.float32)
    mup_p[:PROJ] = f(inputs["L0_attn_mup_vector"]).astype(np.float32)
    hp["mup"] = _pack_cols(mup_p)
    hp["lnw"] = _pack_cols(f(inputs["L0_attn_ln_w"]).astype(np.float32))
    hp["nw"] = _pack_cols(f(inputs["L0_attn_norm_w"]).astype(np.float32))
    cw = f(inputs["L0_attn_conv_w"]).astype(np.float32)
    hp["convw"] = np.ascontiguousarray(
        cw.reshape(NPT_C, 128, 4).transpose(1, 0, 2).reshape(128, NPT_C * 4))
    hp["alog"] = f(inputs["L0_attn_A_log"]).astype(np.float32).reshape(NH, 1)
    hp["dtb"] = f(inputs["L0_attn_dt_bias"]).astype(np.float32).reshape(NH, 1)
    hp["Drow"] = f(inputs["L0_attn_D"]).astype(np.float32).reshape(1, NH)
    x = f(inputs["L0_attn_hidden_states"]).astype(np.float32).reshape(8192, H)
    hp["xs"] = [np.ascontiguousarray(x[c * TPC:(c + 1) * TPC]) for c in range(8)]
    return hp


def _np_consts():
    return {
        "c_if": np.eye(128, dtype=np.float32),
        "c_ib": np.eye(128, dtype=BF),
        "c_ocb": np.ones((128, 1), dtype=BF),
        "c_orf": np.ones((1, 128), dtype=np.float32),
        "c_um": np.triu(np.ones((128, 128), np.float32)),
    }



def _emit_norm_half(C, src_dram, hf, kind, PS, lnw_sb=None):
    nc = C.nc
    C.wtok(C.w, C.latest("sV"), C.latest("sP"), C.latest("os"))
    t_ld = C.dma(C.w, C.nrm.rearrange("p (a b) -> p a b", b=512),
                 src_dram[:, :, hf * 512:(hf + 1) * 512].transpose([1, 0, 2]),
                 "ln")
    C.wtok(C.v, t_ld)
    pe_toks = []
    for pt in range(NKT):
        b = (pt % 2) * 1024
        xcols = C.nrm[:, pt * 512:(pt + 1) * 512]
        if pt >= 2:
            C.wtok(C.v, pe_toks[pt - 2])
        tv = C.inc(C.v.tensor_copy(C.xb2[:, b:b + 512], xcols), "sV")
        C.wtok(C.s, tv)
        ts = C.inc(C.s.activation(C.xb2[:, b + 512:b + 1024],
                                  C.xb2[:, b:b + 512], AF.Square), "sA")
        C.wtok(C.t, ts)
        nc.tensor.matmul(PS[6][0:1, :], C.c_ocb, C.xb2[:, b:b + 512],
                         start=(pt == 0), stop=(pt == NKT - 1))
        i = nc.tensor.matmul(PS[5][0:1, :], C.c_ocb,
                             C.xb2[:, b + 512:b + 1024],
                             start=(pt == 0), stop=(pt == NKT - 1))
        pe_toks.append(C.inc(i, "sP"))
    m = C.row_m
    r = C.row_r
    mr = C.row_mr
    tmp = C.row_tmp
    C.wtok(C.v, pe_toks[-1])
    C.v.tensor_scalar(m, PS[6][0:1, :], 1.0 / H, None, ALU.mult)
    C.v.tensor_scalar(tmp, PS[5][0:1, :], 1.0 / H, None, ALU.mult)
    C.v.tensor_tensor(r, m, m, ALU.mult)
    C.v.tensor_tensor(tmp, tmp, r, ALU.subtract)
    if kind == "ln":
        tv = C.inc(C.v.tensor_scalar(tmp, tmp, 1.0, EPS, ALU.mult, ALU.add),
                   "sV")
    else:
        tv = C.inc(C.v.tensor_scalar(tmp, tmp, 1.0 + EPS, EPS * EPS,
                                     ALU.mult, ALU.add), "sV")
    C.wtok(C.s, tv)
    C.s.activation(tmp, tmp, AF.Ln)
    ta = C.inc(C.s.activation(r, tmp, AF.Exp, scale=-0.5), "sA")
    C.wtok(C.v, ta)
    tv = C.inc(C.v.tensor_tensor(mr, m, r, ALU.mult), "sV")
    C.wtok(C.t, tv)
    nc.tensor.matmul(PS[7][:, :], C.c_orf, r, start=True, stop=True)
    tp = C.inc(nc.tensor.matmul(PS[5][:, :], C.c_orf, mr,
                                start=True, stop=True), "sP")
    C.wtok(C.v, tp)
    last = None
    for pt in range(NKT):
        xcols = C.nrm[:, pt * 512:(pt + 1) * 512]
        dst = C.A[:, pt * 1024 + hf * 512: pt * 1024 + hf * 512 + 512]
        C.v.tensor_tensor(C.t1_512, xcols, PS[7][:, :], ALU.mult)
        if kind == "ln":
            last = C.inc(C.v.tensor_tensor(dst, C.t1_512, PS[5][:, :],
                                           ALU.subtract), "sV")
        else:
            C.v.tensor_tensor(C.t1_512, C.t1_512, PS[5][:, :], ALU.subtract)
            last = C.inc(C.v.tensor_scalar(dst, C.t1_512,
                                           lnw_sb[:, pt:pt + 1], None,
                                           ALU.mult), "sV")
    return last


def _emit_ffn(C, src_dram, dst_dram, PS, Wg, Wu, Wd, wdbf, h2d, final_out=None,
              a_ready=None):
    nc = C.nc
    ev = {}
    wtoks = {}
    for mi in range(NMI_G + 1):
        if mi < NMI_G:
            if mi >= 2:
                C.wtok(C.g, ev[(mi - 2, "pe")])
            wtoks[mi] = (
                C.dma(C.g, C.wgt[mi % 2][:, :],
                      Wg[mi].rearrange("p a b -> p (a b)"), f"wg{mi % 2}"),
                C.dma(C.g, C.wut[mi % 2][:, :],
                      Wu[mi].rearrange("p a b -> p (a b)"), f"wu{mi % 2}"))
        if mi >= 1:
            j = mi - 1
            C.wtok(C.t, wtoks[j][0], wtoks[j][1])
            if j < 2:
                C.wtok(C.t, a_ready)
            else:
                C.wtok(C.t, ev[(j - 2, "dve")])
            bk = (j % 2) * 4
            for kt in range(NKT):
                lw_g = C.wgt[j % 2][:, kt * 128:(kt + 1) * 128]
                lw_u = C.wut[j % 2][:, kt * 128:(kt + 1) * 128]
                st = (kt == 0)
                sp = (kt == NKT - 1)
                nc.tensor.matmul(PS[bk + 0][:, :], lw_g,
                                 C.A[:, kt * 1024:kt * 1024 + 512],
                                 start=st, stop=sp)
                nc.tensor.matmul(PS[bk + 1][:, :], lw_g,
                                 C.A[:, kt * 1024 + 512:kt * 1024 + 1024],
                                 start=st, stop=sp)
                nc.tensor.matmul(PS[bk + 2][:, :], lw_u,
                                 C.A[:, kt * 1024:kt * 1024 + 512],
                                 start=st, stop=sp)
                i = nc.tensor.matmul(PS[bk + 3][:, :], lw_u,
                                     C.A[:, kt * 1024 + 512:kt * 1024 + 1024],
                                     start=st, stop=sp)
            ev[(j, "pe")] = C.inc(i, "sP")
            C.wtok(C.s, ev[(j, "pe")])
            if j >= 1:
                C.wtok(C.s, ev[(j - 1, "dve")])
            C.s.activation(C.tg1, PS[bk + 0][:, :], AF.Silu,
                           scale=C.sb_sgate[:, j:j + 1])
            ta = C.inc(C.s.activation(C.tg2, PS[bk + 1][:, :], AF.Silu,
                                      scale=C.sb_sgate[:, j:j + 1]), "sA")
            C.wtok(C.v, ta)
            if j >= 1:
                C.wtok(C.v, ev[(j - 1, "st")])
            C.v.scalar_tensor_tensor(C.B[:, j * 512:(j + 1) * 512],
                                     PS[bk + 2][:, :], C.sb_sup[:, j:j + 1],
                                     C.tg1, ALU.mult, ALU.mult)
            tv = C.inc(C.v.scalar_tensor_tensor(C.h2t, PS[bk + 3][:, :],
                                                C.sb_sup[:, j:j + 1], C.tg2,
                                                ALU.mult, ALU.mult), "sV")
            ev[(j, "dve")] = tv
            C.wtok(C.w, tv)
            ev[(j, "st")] = C.dma(C.w, h2d[j], C.h2t, "h2s")

    if _UPTO == "gateup":
        C.wtok(C.g, ev[(NMI_G - 1, "dve")], C.latest("h2s"))
        for j in range(NMI_G):
            d = (C.dbg_dumps[0][j] if j < 32 else
                 C.dbg_dumps[1][j - 32] if j < 66 else
                 C.dbg_dumps[2][j - 66])
            C.dma(C.g, d[:, 0:512], C.B[:, j * 512:(j + 1) * 512], "dump")
            C.dma(C.g, d[:, 512:1024], h2d[j], "dump")
        return None
    if _UPTO == "down0":
        halves = (0,)
    else:
        halves = (0, 1)

    last_store = None
    last_tr = None
    prev_half_dve = ev[(NMI_G - 1, "dve")]
    prev_half_pe = ev[(NMI_G - 1, "pe")]
    for hf in halves:
        t_h2 = None
        if hf == 1:
            C.wtok(C.w, C.latest("sP"), C.latest("h2s"), C.latest("cw"))
            t_h2 = C.dma(C.w, C.B.ap().rearrange("p (a b) -> p a b", b=512),
                         h2d[:, :, :].transpose([1, 0, 2]), "h2l")
        wdk = {}
        evd = {}
        for o in range(32 + 1):
            if o < 32:
                if o < 2:
                    C.wtok(C.g, prev_half_pe)
                else:
                    C.wtok(C.g, evd[(o - 2, "pe")])
                    if hf == 0 and wdbf is not None:
                        C.wtok(C.g, C.latest("cw"))
                if hf == 0:
                    wdk[o] = C.dma(C.g, C.wdt[o % 2],
                                   Wd[o].rearrange("p a b -> p (a b)"),
                                   f"wd{o % 2}")
                else:
                    wdk[o] = C.dma(C.w, C.wdt[o % 2], wdbf[o], f"wdl{o % 2}")
            if o >= 1:
                j = o - 1
                if hf == 0 and wdbf is not None:
                    C.wtok(C.w, wdk[j])
                    C.dma(C.w, wdbf[j], C.wdt[j % 2], "cw")
                C.wtok(C.t, wdk[j], t_h2)
                if j < 2:
                    C.wtok(C.t, prev_half_dve)
                else:
                    C.wtok(C.t, evd[(j - 2, "dve")])
                bk = j % 2
                for k in range(NMI_G):
                    i = nc.tensor.matmul(PS[bk][:, :],
                                         C.wdt[j % 2][:, k * 128:(k + 1) * 128],
                                         C.B[:, k * 512:(k + 1) * 512],
                                         start=(k == 0), stop=(k == NMI_G - 1))
                evd[(j, "pe")] = C.inc(i, "sP")
                t_rx = C.dma(C.w, C.rx, src_dram[j, :, hf * 512:(hf + 1) * 512],
                             "rx")
                C.wtok(C.v, evd[(j, "pe")], t_rx, last_store, last_tr)
                C.v.tensor_scalar(C.t1_512, PS[bk][:, :],
                                  C.sb_sdown[:, j:j + 1], 0.5,
                                  ALU.mult, ALU.mult)
                tv = C.inc(C.v.tensor_tensor(C.ost, C.t1_512, C.rx, ALU.add),
                           "sV")
                evd[(j, "dve")] = tv
                if final_out is None:
                    C.wtok(C.w, tv)
                    last_store = C.dma(C.w,
                                       dst_dram[j, :, hf * 512:(hf + 1) * 512],
                                       C.ost, "os")
                else:
                    C.wtok(C.t, tv)
                    for tq in range(4):
                        bkp = 2 + (tq % 2)
                        if tq >= 2:
                            C.wtok(C.t, evd[(j, "tr%d" % (tq - 2))])
                        i = nc.tensor.transpose(
                            PS[bkp].ap()[:, 0:128],
                            C.ost[:, tq * 128:(tq + 1) * 128], C.c_if)
                        last_tr = C.inc(i, "sP")
                        C.wtok(C.v, last_tr, last_store)
                        tve = C.inc(C.v.tensor_copy(
                            C.tstg[:, tq * 128:(tq + 1) * 128],
                            PS[bkp].ap()[:, 0:128]), "sV")
                        evd[(j, "tr%d" % tq)] = tve
                        C.wtok(C.w, tve)
                        r0 = hf * 512 + tq * 128
                        last_store = C.dma(
                            C.w, final_out[r0:r0 + 128, j * 128:(j + 1) * 128],
                            C.tstg[:, tq * 128:(tq + 1) * 128], "os")
        prev_half_dve = evd[(31, "dve")]
        prev_half_pe = evd[(31, "pe")]
    return last_store


def _alloc_common(C, nc, final_path=False):
    C.A = nc.alloc_sbuf_tensor("A", [128, 32 * 1024], BF16)
    C.Abf = C.A.ap()
    C.B = nc.alloc_sbuf_tensor("Bh", [128, NMI_G * 512], BF16)
    C.Bf32 = C.B.ap().bitcast(F32)
    C.nrm = C.Bf32[:, 0:NKT * 512]
    C.wdt = [C.Abf[:, 0:INTER], C.Abf[:, INTER:2 * INTER]]
    C.wgt = [nc.alloc_sbuf_tensor(f"wgt{i}", [128, H], BF16) for i in range(2)]
    C.wut = [nc.alloc_sbuf_tensor(f"wut{i}", [128, H], BF16) for i in range(2)]
    C.xb2 = nc.alloc_sbuf_tensor("xb2", [128, 2048], BF16).ap()
    C.t1_512 = nc.alloc_sbuf_tensor("t1_512", [128, 512], F32).ap()
    C.rx = nc.alloc_sbuf_tensor("rx", [128, 512], F32).ap()
    C.ost = nc.alloc_sbuf_tensor("ost", [128, 512], F32).ap()
    C.tg1 = nc.alloc_sbuf_tensor("tg1", [128, 512], F32).ap()
    C.tg2 = nc.alloc_sbuf_tensor("tg2", [128, 512], F32).ap()
    C.h2t = nc.alloc_sbuf_tensor("h2t", [128, 512], BF16).ap()
    if final_path:
        C.tstg = nc.alloc_sbuf_tensor("tstg", [128, 512], F32).ap()
    C.row_m = C.tg1[0:1, :]
    C.row_tmp = C.tg2[0:1, :]
    C.row_r = C.rx[0:1, :]
    C.row_mr = C.ost[0:1, :]


def _final_wait(C):
    for s in sorted(C.cnt.keys()):
        C.wtok(C.g, C.latest(s))



def _build_L1():
    nc = bass.Bass("TRN2", target_bir_lowering=False, detect_race_conditions=_DETECT)
    x_in = nc.dram_tensor("x_in", [TPC, H], F32, kind="ExternalInput")
    Wg = nc.dram_tensor("Wg", [NMI_G, 128, NKT, 128], I32, kind="ExternalInput")
    Wu = nc.dram_tensor("Wu", [NMI_G, 128, NKT, 128], I32, kind="ExternalInput")
    Wd = nc.dram_tensor("Wd", [32, 128, NMI_G, 128], I32, kind="ExternalInput")
    Wip = nc.dram_tensor("Wip", [NMI_P, 128, NKT, 128], F32, kind="ExternalInput")
    sgate = nc.dram_tensor("sgate", [128, NMI_G], F32, kind="ExternalInput")
    sup = nc.dram_tensor("sup", [128, NMI_G], F32, kind="ExternalInput")
    sdown = nc.dram_tensor("sdown", [128, 32], F32, kind="ExternalInput")
    mup = nc.dram_tensor("mup", [128, NMI_P], F32, kind="ExternalInput")
    lnw = nc.dram_tensor("lnw", [128, 32], F32, kind="ExternalInput")
    cs = _np_consts()
    c_if_d = nc.inline_tensor(cs["c_if"], name="c_if")
    c_ocb_d = nc.inline_tensor(cs["c_ocb"], name="c_ocb")
    c_orf_d = nc.inline_tensor(cs["c_orf"], name="c_orf")

    R1o = nc.dram_tensor("R1o", [32, 128, TPC], F32, kind="ExternalOutput")
    Zo = nc.dram_tensor("Zo", [32, 128, TPC], F32, kind="ExternalOutput")
    XBCo = nc.dram_tensor("XBCo", [NPT_C, 128, TPC], F32, kind="ExternalOutput")
    DTo = nc.dram_tensor("DTo", [NH, TPC], F32, kind="ExternalOutput")

    Xf = nc.dram_tensor("Xf", [32, 128, TPC], F32)
    h2d = nc.dram_tensor("h2d", [NMI_G, 128, 512], BF16)
    wdbf = nc.dram_tensor("wdbf", [32, 128, INTER], BF16)

    C = Ctx(nc)
    _alloc_common(C, nc)
    C.sb_sgate = nc.alloc_sbuf_tensor("sb_sgate", [128, NMI_G], F32).ap()
    C.sb_sup = nc.alloc_sbuf_tensor("sb_sup", [128, NMI_G], F32).ap()
    C.sb_sdown = nc.alloc_sbuf_tensor("sb_sdown", [128, 32], F32).ap()
    C.sb_mup = nc.alloc_sbuf_tensor("sb_mup", [128, NMI_P], F32).ap()
    C.sb_lnw = nc.alloc_sbuf_tensor("sb_lnw", [128, 32], F32).ap()
    C.c_if = nc.alloc_sbuf_tensor("sb_c_if", [128, 128], F32).ap()
    C.c_ocb = nc.alloc_sbuf_tensor("sb_c_ocb", [128, 1], BF16).ap()
    C.c_orf = nc.alloc_sbuf_tensor("sb_c_orf", [1, 128], F32).ap()
    ipst = nc.alloc_sbuf_tensor("ipst", [128, 1024], F32).ap()
    dtst = ipst[0:64, :]

    PS = [nc.alloc_psum_tensor(f"ps{i}", [128, 512], F32) for i in range(8)]

    with nc.Block() as block:
        @block.gpsimd
        def _(g):
            t_c = None
            for dst, src in ((C.sb_sgate, sgate), (C.sb_sup, sup),
                             (C.sb_sdown, sdown), (C.sb_mup, mup),
                             (C.sb_lnw, lnw), (C.c_if, c_if_d),
                             (C.c_ocb, c_ocb_d), (C.c_orf, c_orf_d)):
                t_c = C.dma(C.w, dst, src[:, :], "c")
            if _UPTO == "consts":
                _final_wait(C)
                return

            xt = [C.Bf32[:, 0:4096], C.Bf32[:, 4096:8192]]
            stg = [C.Bf32[:, 8192:12288], C.Bf32[:, 12288:16384]]
            ld = {}
            tr_tok = {}
            st_tok = {}
            for tc in range(8):
                par = tc % 2
                if tc >= 2:
                    C.wtok(C.w, tr_tok[tc - 2])
                ld[tc] = C.dma(C.w, xt[par], x_in[tc * 128:(tc + 1) * 128, :],
                               f"lx{par}")
                C.wtok(C.t, ld[tc], t_c)
                if tc >= 2:
                    C.wtok(C.t, st_tok[tc - 2])
                ptoks = []
                for p in range(NKT):
                    bkp = p % 4
                    if p >= 4:
                        C.wtok(C.t, ptoks[p - 4])
                    i = nc.tensor.transpose(PS[bkp].ap()[:, 0:128],
                                            xt[par][:, p * 128:(p + 1) * 128],
                                            C.c_if)
                    tp = C.inc(i, "sP")
                    C.wtok(C.v, tp)
                    ptoks.append(C.inc(C.v.tensor_copy(
                        stg[par][:, p * 128:(p + 1) * 128],
                        PS[bkp].ap()[:, 0:128]), "sV"))
                tr_tok[tc] = C.latest("sP")
                C.wtok(C.w, ptoks[-1])
                st_tok[tc] = C.dma(
                    C.w, Xf[:, :, tc * 128:(tc + 1) * 128].transpose([1, 0, 2]),
                    stg[par].rearrange("p (a b) -> p a b", b=128), f"sx{par}")
            C.wtok(C.w, st_tok[6], st_tok[7])
            if _UPTO == "entry":
                C.wtok(C.g, st_tok[6], st_tok[7])
                for kt in range(32):
                    C.dma(C.g, R1o[kt], Xf[kt], "dump")
                _final_wait(C)
                return

            for hf in (0, 1):
                nrm_tok = _emit_norm_half(C, Xf, hf, "ln", PS)
            if _UPTO == "norm":
                C.wtok(C.g, nrm_tok)
                for kt in range(32):
                    C.dma(C.g, R1o[kt],
                          C.A.ap()[:, kt * 1024:(kt + 1) * 1024], "dump")
                _final_wait(C)
                return
            C.dbg_dumps = (R1o, XBCo, Zo)
            _emit_ffn(C, Xf, R1o, PS, Wg, Wu, Wd, wdbf, h2d, a_ready=nrm_tok)
            if _UPTO in ("ffn", "gateup", "down0"):
                _final_wait(C)
                return

            for hf in (0, 1):
                nrm_tok = _emit_norm_half(C, R1o, hf, "ln_rms", PS,
                                          lnw_sb=C.sb_lnw)
            if _UPTO == "mixnorm":
                _final_wait(C)
                return

            wk = {}
            evp = {}
            last_store = None
            for mi in range(NMI_P + 1):
                if mi < NMI_P:
                    if mi >= 2:
                        C.wtok(C.g, evp[(mi - 2, "pe")])
                    wk[mi] = C.dma(C.g, C.wgt[mi % 2][:, :],
                                   Wip[mi].rearrange("p a b -> p (a b)"),
                                   f"wg{mi % 2}")
                if mi >= 1:
                    j = mi - 1
                    C.wtok(C.t, wk[j])
                    if j < 2:
                        C.wtok(C.t, nrm_tok)
                    else:
                        C.wtok(C.t, evp[(j - 2, "dve")])
                    b0 = (j % 2) * 2
                    for kt in range(NKT):
                        lw = C.wgt[j % 2][:, kt * 128:(kt + 1) * 128]
                        st = (kt == 0)
                        sp = (kt == NKT - 1)
                        nc.tensor.matmul(PS[b0][:, :], lw,
                                         C.A[:, kt * 1024:kt * 1024 + 512],
                                         start=st, stop=sp)
                        i = nc.tensor.matmul(PS[b0 + 1][:, :], lw,
                                             C.A[:, kt * 1024 + 512:(kt + 1) * 1024],
                                             start=st, stop=sp)
                    evp[(j, "pe")] = C.inc(i, "sP")
                    C.wtok(C.v, evp[(j, "pe")], last_store)
                    if j < 66:
                        C.v.tensor_scalar(ipst[:, 0:512], PS[b0][:, :],
                                          C.sb_mup[:, j:j + 1], None, ALU.mult)
                        tv = C.inc(C.v.tensor_scalar(ipst[:, 512:1024],
                                                     PS[b0 + 1][:, :],
                                                     C.sb_mup[:, j:j + 1],
                                                     None, ALU.mult), "sV")
                        evp[(j, "dve")] = tv
                        C.wtok(C.w, tv)
                        dst = Zo[j] if j < 32 else XBCo[j - 32]
                        last_store = C.dma(C.w, dst, ipst, "os")
                    else:
                        C.v.tensor_scalar(dtst[:, 0:512], PS[b0][0:64, :],
                                          C.sb_mup[0:64, j:j + 1], None,
                                          ALU.mult)
                        tv = C.inc(C.v.tensor_scalar(dtst[:, 512:1024],
                                                     PS[b0 + 1][0:64, :],
                                                     C.sb_mup[0:64, j:j + 1],
                                                     None, ALU.mult), "sV")
                        evp[(j, "dve")] = tv
                        C.wtok(C.w, tv)
                        last_store = C.dma(C.w, DTo[:, :], dtst, "os")
            _final_wait(C)
    return nc



def _build_L2():
    nc = bass.Bass("TRN2", target_bir_lowering=False, detect_race_conditions=_DETECT)
    xbc = nc.dram_tensor("xbc", [NPT_C, 128, TPC + 3], F32, kind="ExternalInput")
    dt_in = nc.dram_tensor("dt_in", [NH, TPC], F32, kind="ExternalInput")
    alog = nc.dram_tensor("alog", [NH, 1], F32, kind="ExternalInput")
    dtb_in = nc.dram_tensor("dtb_in", [NH, 1], F32, kind="ExternalInput")
    Drow = nc.dram_tensor("Drow", [1, NH], F32, kind="ExternalInput")
    convw = nc.dram_tensor("convw", [128, NPT_C * 4], F32, kind="ExternalInput")
    cs = _np_consts()
    cst = {k: nc.inline_tensor(v, name=k) for k, v in cs.items()}

    Yo = nc.dram_tensor("Yo", [NCH, 128, H], F32, kind="ExternalOutput")
    CMo = nc.dram_tensor("CMo", [128, TPC], BF16, kind="ExternalOutput")
    EGo = nc.dram_tensor("EGo", [NCH, 128, NH], F32, kind="ExternalOutput")
    GBo = nc.dram_tensor("GBo", [128, H + NH], F32, kind="ExternalOutput")

    arow_d = nc.dram_tensor("arow_d", [NH, Q], F32)

    C = Ctx(nc)
    sbt = nc.alloc_sbuf_tensor
    xin = sbt("xin", [128, NPT_C * 131], F32).ap()
    cv = sbt("cv", [128, NPT_C * 128], F32).ap()
    xs = sbt("xs", [128, 32 * 128], F32).ap()
    Bm = sbt("Bm", [128, 128], BF16).ap()
    BmT = sbt("BmT", [128, 128], BF16).ap()
    CMall = sbt("CMall", [128, TPC], BF16).ap()
    G2m = sbt("G2m", [128, 128], BF16).ap()
    xsT = sbt("xsT", [128, H], BF16).ap()
    X3 = sbt("X3", [128, H], BF16).ap()
    SR = sbt("SR", [128, H], F32).ap()
    SRb = sbt("SRb", [128, H], BF16).ap()
    yT = sbt("yT", [128, H], F32).ap()
    wr = sbt("wr", [128, 4 * 128], BF16).ap()
    Et = sbt("Et", [128, 4 * 128 + 256], F32).ap()
    hs = sbt("hs", [NH, 8 * TPC], F32).ap()
    sp_ = hs[:, 0:TPC]
    gcum = hs[:, TPC:2 * TPC]
    arel = hs[:, 2 * TPC:3 * TPC]
    erow = hs[:, 3 * TPC:4 * TPC]
    eg = hs[:, 4 * TPC:5 * TPC]
    w3 = hs[:, 5 * TPC:6 * TPC]
    htmp = hs[:, 6 * TPC:7 * TPC]
    zeros = hs[:, 7 * TPC:8 * TPC]
    dtsb = sbt("dtsb", [NH, TPC], F32).ap()
    acol = sbt("acol", [NH, 2], F32).ap()
    tcol = sbt("tcol", [128, 8 * NH], F32).ap()
    Db = sbt("Db", [128, NH], F32).ap()
    arow = sbt("arow", [1, NH * Q], F32).ap()
    prow = sbt("prow", [1, NH], F32).ap()
    gbst = sbt("gbst", [128, H + NH], F32).ap()
    c_if = sbt("sb_c_if", [128, 128], F32).ap()
    c_ib = sbt("sb_c_ib", [128, 128], BF16).ap()
    c_orf = sbt("sb_c_orf", [1, 128], F32).ap()
    c_um = sbt("sb_c_um", [128, 128], F32).ap()
    convw_sb = sbt("sb_convw", [128, NPT_C * 4], F32).ap()

    PS = [nc.alloc_psum_tensor(f"ps{i}", [128, 512], F32) for i in range(8)]
    PSb4 = PS[4].ap().bitcast(BF16)

    with nc.Block() as block:
        @block.gpsimd
        def _(g):
            for dst, src in ((c_if, cst["c_if"]), (c_ib, cst["c_ib"]),
                             (c_orf, cst["c_orf"]), (c_um, cst["c_um"]),
                             (convw_sb, convw), (dtsb, dt_in)):
                C.dma(C.w, dst, src[:, :], "c")
            C.dma(C.w, acol[:, 1:2], dtb_in[:, :], "c")
            C.dma(C.w, acol[:, 0:1], alog[:, :], "c")
            t_c = C.dma(C.w, prow[0:1, 0:NH], Drow[:, :], "c")

            C.wtok(C.t, t_c)
            tp = C.inc(nc.tensor.matmul(PS[4].ap()[:, 0:NH], c_orf,
                                        prow[0:1, 0:NH], start=True,
                                        stop=True), "sP")
            C.wtok(C.v, tp)
            tv_db = C.inc(C.v.tensor_copy(Db, PS[4].ap()[:, 0:NH]), "sV")

            C.wtok(C.v, t_c)
            C.v.memset(zeros, 0.0)
            tv = C.inc(C.v.tensor_scalar(htmp, dtsb, acol[:, 1:2], None,
                                         ALU.add), "sV")
            C.wtok(C.s, tv)
            C.s.activation(sp_, htmp, AF.Abs)
            C.s.activation(sp_, sp_, AF.Exp, scale=-1.0)
            ta = C.inc(C.s.activation(sp_, sp_, AF.Ln, bias=1.0), "sA")
            C.wtok(C.v, ta)
            C.v.tensor_scalar(htmp, htmp, 0.0, None, ALU.max)
            tv = C.inc(C.v.tensor_tensor(sp_, sp_, htmp, ALU.add), "sV")
            C.wtok(C.s, tv)
            ta = C.inc(C.s.activation(acol[:, 0:1], acol[:, 0:1], AF.Exp),
                       "sA")
            C.wtok(C.v, ta)
            C.v.tensor_scalar(htmp, sp_, acol[:, 0:1], -1.0, ALU.mult,
                              ALU.mult)
            C.v.tensor_tensor_scan(gcum, htmp, zeros, 0.0, ALU.add, ALU.add)
            for c in range(NCH):
                gp = zeros[:, 0:1] if c == 0 else gcum[:, c * Q - 1:c * Q]
                C.v.tensor_scalar(arel[:, c * Q:(c + 1) * Q],
                                  gcum[:, c * Q:(c + 1) * Q], gp, None,
                                  ALU.subtract)
            for c in range(NCH):
                ae = arel[:, c * Q + Q - 1:c * Q + Q]
                C.v.tensor_scalar(w3[:, c * Q:(c + 1) * Q],
                                  arel[:, c * Q:(c + 1) * Q], ae, -1.0,
                                  ALU.subtract, ALU.mult)
            tv = C.inc(C.v.engine_nop(), "sV")
            C.wtok(C.s, tv)
            C.s.activation(erow, arel, AF.Exp)
            C.s.activation(eg, gcum, AF.Exp)
            ta = C.inc(C.s.activation(w3, w3, AF.Exp), "sA")
            C.wtok(C.v, ta)
            C.inc(C.v.tensor_tensor(w3, w3, sp_, ALU.mult), "sV")
            C.v.memset(SR, 0.0)
            sr_tok = C.inc(C.v.memset(SRb, 0.0), "sV")

            if _UPTO == "dtprep":
                _final_wait(C)
                return

            for c in range(NCH):
                C.wtok(C.w, C.latest("sV"), C.latest("sP"))
                C.wtok(C.t, C.latest("sV"), C.latest("sA"))
                C.wtok(C.s, C.latest("sP"))
                C.wtok(C.v, C.latest("ys"))

                t_x = C.dma(C.w, xin.rearrange("p (a b) -> p a b", b=131),
                            xbc[:, :, c * Q:c * Q + 131].transpose([1, 0, 2]),
                            "lx")
                t_ar = C.dma(C.w, arow_d[:, :], arel[:, c * Q:(c + 1) * Q],
                             "ar")
                C.wtok(C.w, t_ar)
                t_ar2 = C.dma(C.w, arow,
                              arow_d.ap().rearrange("a b -> (a b)")[None, :],
                              "ar")

                for src, slot in ((sp_, 0), (arel, 1), (erow, 2), (eg, 3),
                                  (w3, 5)):
                    i = nc.tensor.transpose(
                        PS[6].ap()[:, slot * 64:slot * 64 + 64],
                        src[:, c * Q:(c + 1) * Q], c_if[0:64, 0:64])
                    ttok = C.inc(i, "sP")
                i = nc.tensor.transpose(PS[6].ap()[0:1, 7 * 64:8 * 64],
                                        erow[:, c * Q + Q - 1:c * Q + Q],
                                        c_if[0:64, 0:64])
                ttok = C.inc(i, "sP")
                C.wtok(C.v, ttok)
                C.v.tensor_scalar(tcol[:, 64:128], PS[6].ap()[:, 64:128],
                                  -1.0, None, ALU.mult)
                for slot in (0, 2, 3, 5):
                    C.v.tensor_copy(tcol[:, slot * 64:(slot + 1) * 64],
                                    PS[6].ap()[:, slot * 64:(slot + 1) * 64])
                tv = C.inc(C.v.tensor_copy(prow[0:1, 0:NH],
                                           PS[6].ap()[0:1, 7 * 64:8 * 64]),
                           "sV")
                C.wtok(C.t, tv)
                i = nc.tensor.matmul(PS[6].ap()[:, 6 * 64:7 * 64], c_orf,
                                     prow[0:1, 0:NH], start=True, stop=True)
                tp = C.inc(i, "sP")
                C.wtok(C.v, tp)
                tv_tc = C.inc(C.v.tensor_copy(tcol[:, 6 * 64:7 * 64],
                                              PS[6].ap()[:, 6 * 64:7 * 64]),
                              "sV")
                C.wtok(C.w, tv_tc)
                C.dma(C.w, EGo[c], tcol[:, 3 * 64:4 * 64], "eg")

                C.wtok(C.v, t_x)
                for pt in range(NPT_C):
                    a = cv[:, pt * 128:(pt + 1) * 128]
                    C.v.tensor_scalar(a, xin[:, pt * 131:pt * 131 + 128],
                                      convw_sb[:, pt * 4:pt * 4 + 1], None,
                                      ALU.mult)
                    for k in (1, 2, 3):
                        C.v.scalar_tensor_tensor(
                            a, xin[:, pt * 131 + k:pt * 131 + k + 128],
                            convw_sb[:, pt * 4 + k:pt * 4 + k + 1], a,
                            ALU.mult, ALU.add)
                cvtok = C.inc(C.v.engine_nop(), "sV")
                C.wtok(C.s, cvtok)
                for pt in range(32):
                    C.s.activation(xs[:, pt * 128:(pt + 1) * 128],
                                   cv[:, pt * 128:(pt + 1) * 128], AF.Silu)
                C.s.activation(Bm, cv[:, 32 * 128:33 * 128], AF.Silu)
                ta_si = C.inc(C.s.activation(CMall[:, c * Q:(c + 1) * Q],
                                             cv[:, 33 * 128:34 * 128],
                                             AF.Silu), "sA")

                C.wtok(C.t, ta_si, tv_db)
                i = nc.tensor.matmul(PS[5].ap()[:, 0:128], Bm,
                                     CMall[:, c * Q:(c + 1) * Q],
                                     start=True, stop=True)
                tp = C.inc(i, "sP")
                C.wtok(C.v, tp)
                tvg = C.inc(C.v.tensor_tensor(G2m, PS[5].ap()[:, 0:128], c_um,
                                              ALU.mult), "sV")
                C.wtok(C.t, tvg)
                i = nc.tensor.transpose(PSb4[:, 0:128], Bm, c_ib)
                tp = C.inc(i, "sP")
                C.wtok(C.v, tp)
                tvb = C.inc(C.v.tensor_copy(BmT, PSb4[:, 0:128]), "sV")

                xstoks = []
                for p in range(32):
                    bkp = 6 + (p % 2)
                    if p >= 2:
                        C.wtok(C.t, xstoks[p - 2])
                    i = nc.tensor.transpose(PS[bkp].ap()[:, 0:128],
                                            xs[:, p * 128:(p + 1) * 128],
                                            c_if)
                    tp = C.inc(i, "sP")
                    C.wtok(C.v, tp)
                    xstoks.append(C.inc(C.v.tensor_copy(
                        xsT[:, p * 128:(p + 1) * 128],
                        PS[bkp].ap()[:, 0:128]), "sV"))

                for r in range(NH):
                    C.v.tensor_scalar(X3[:, r * 64:(r + 1) * 64],
                                      xsT[:, r * 64:(r + 1) * 64],
                                      tcol[:, 5 * 64 + r:5 * 64 + r + 1],
                                      None, ALU.mult)
                tv_x3 = C.inc(C.v.engine_nop(), "sV")

                C.wtok(C.t, t_ar2, tvb)
                t1_tok = {}
                e_tok = {}
                wr_tok = {}
                y_ev = {}

                arg_tok = {}

                def emit_T1(r):
                    qq = r % 4
                    bkq = 4 + qq // 2
                    off = (qq % 2) * 128
                    if r >= 4:
                        C.wtok(C.t, arg_tok[r - 4])
                    i = nc.tensor.matmul(PS[bkq].ap()[:, off:off + 128],
                                         c_orf, arow[0:1, r * Q:(r + 1) * Q],
                                         start=True, stop=True)
                    t1_tok[r] = C.inc(i, "sP")

                def emit_EW(r):
                    qq = r % 4
                    bkq = 4 + qq // 2
                    off = (qq % 2) * 128
                    C.wtok(C.v, t1_tok[r])
                    if r >= 4:
                        C.wtok(C.v, wr_tok[r - 4])
                    arg_tok[r] = C.inc(C.v.tensor_scalar(
                        Et[:, qq * 128:(qq + 1) * 128],
                        PS[bkq].ap()[:, off:off + 128],
                        tcol[:, 64 + r:64 + r + 1], 0.0,
                        ALU.add, ALU.min), "sV")
                    C.wtok(C.s, arg_tok[r])
                    e_tok[r] = C.inc(C.s.activation(
                        Et[:, qq * 128:(qq + 1) * 128],
                        Et[:, qq * 128:(qq + 1) * 128], AF.Exp), "sA")
                    C.wtok(C.v, e_tok[r])
                    wr_tok[r] = C.inc(C.v.scalar_tensor_tensor(
                        wr[:, qq * 128:(qq + 1) * 128],
                        Et[:, qq * 128:(qq + 1) * 128],
                        tcol[:, r:r + 1], G2m, ALU.mult, ALU.mult), "sV")

                def emit_intra(r):
                    qq = r % 4
                    gidx = r // 8
                    hh = r % 8
                    C.wtok(C.t, wr_tok[r])
                    if hh == 0 and gidx >= 2:
                        C.wtok(C.t, y_ev[gidx - 2])
                    i = nc.tensor.matmul(
                        PS[gidx % 2].ap()[:, hh * 64:(hh + 1) * 64],
                        wr[:, qq * 128:(qq + 1) * 128],
                        xsT[:, r * 64:(r + 1) * 64], start=True, stop=True)
                    C.inc(i, "sP")
                    if hh == 7:
                        if gidx >= 2:
                            C.wtok(C.t, y_ev[gidx - 2])
                        i = nc.tensor.matmul(
                            PS[2 + gidx % 2].ap()[:, :],
                            CMall[:, c * Q:(c + 1) * Q],
                            SRb[:, gidx * 512:(gidx + 1) * 512],
                            start=True, stop=True)
                        tp_g = C.inc(i, "sP")
                        C.wtok(C.v, tp_g)
                        for h2 in range(8):
                            rr = gidx * 8 + h2
                            C.v.tensor_scalar(
                                Et[:, 512:576],
                                PS[2 + gidx % 2].ap()[:, h2 * 64:(h2 + 1) * 64],
                                tcol[:, 2 * 64 + rr:2 * 64 + rr + 1], None,
                                ALU.mult)
                            C.v.tensor_tensor(
                                Et[:, 576:640],
                                PS[gidx % 2].ap()[:, h2 * 64:(h2 + 1) * 64],
                                Et[:, 512:576], ALU.add)
                            C.v.scalar_tensor_tensor(
                                yT[:, rr * 64:(rr + 1) * 64],
                                xsT[:, rr * 64:(rr + 1) * 64],
                                Db[:, rr:rr + 1], Et[:, 576:640],
                                ALU.mult, ALU.add)
                        y_ev[gidx] = C.inc(C.v.engine_nop(), "sV")

                emit_T1(0)
                emit_EW(0)
                emit_T1(1)
                emit_EW(1)
                emit_T1(2)
                emit_EW(2)
                for r in range(NH):
                    if r + 3 < NH:
                        emit_T1(r + 3)
                        emit_EW(r + 3)
                    emit_intra(r)

                C.wtok(C.w, y_ev[7])
                C.dma(C.w, Yo[c], yT, "ys")

                C.wtok(C.t, tv_x3, y_ev[7])
                st_ev = {}
                for gi in range(8):
                    if gi >= 2:
                        C.wtok(C.t, st_ev[gi - 2])
                    i = nc.tensor.matmul(PS[2 + gi % 2].ap()[:, :], BmT,
                                         X3[:, gi * 512:(gi + 1) * 512],
                                         start=True, stop=True)
                    tp = C.inc(i, "sP")
                    C.wtok(C.v, tp)
                    if gi == 0:
                        C.wtok(C.v, sr_tok)
                    for h2 in range(8):
                        rr = gi * 8 + h2
                        C.v.scalar_tensor_tensor(
                            SR[:, rr * 64:(rr + 1) * 64],
                            SR[:, rr * 64:(rr + 1) * 64],
                            tcol[:, 6 * 64 + rr:6 * 64 + rr + 1],
                            PS[2 + gi % 2].ap()[:, h2 * 64:(h2 + 1) * 64],
                            ALU.mult, ALU.add)
                    st_ev[gi] = C.inc(C.v.engine_nop(), "sV")
                sr_tok = C.inc(C.v.tensor_copy(SRb, SR), "sV")

            C.wtok(C.w, C.latest("sA"))
            C.dma(C.w, CMo[:, :], CMall, "os")
            C.wtok(C.v, sr_tok)
            C.inc(C.v.tensor_copy(gbst[:, 0:H], SR), "sV")
            C.wtok(C.t, C.latest("sV"))
            i = nc.tensor.transpose(PS[6].ap()[0:1, 0:NH],
                                    gcum[:, TPC - 1:TPC], c_if[0:64, 0:64])
            tp = C.inc(i, "sP")
            C.wtok(C.v, tp)
            tv = C.inc(C.v.tensor_copy(prow[0:1, 0:NH],
                                       PS[6].ap()[0:1, 0:NH]), "sV")
            C.wtok(C.t, tv)
            i = nc.tensor.matmul(PS[6].ap()[:, 64:128], c_orf,
                                 prow[0:1, 0:NH], start=True, stop=True)
            tp = C.inc(i, "sP")
            C.wtok(C.v, tp)
            tv = C.inc(C.v.tensor_copy(gbst[:, H:H + NH],
                                       PS[6].ap()[:, 64:128]), "sV")
            C.wtok(C.w, tv)
            C.dma(C.w, GBo[:, :], gbst, "os")
            _final_wait(C)
    return nc



def _build_L3():
    nc = bass.Bass("TRN2", target_bir_lowering=False, detect_race_conditions=_DETECT)
    GB8 = nc.dram_tensor("GB8", [8, 128, H + NH], F32, kind="ExternalInput")
    Yi = nc.dram_tensor("Yi", [NCH, 128, H], F32, kind="ExternalInput")
    CMi = nc.dram_tensor("CMi", [128, TPC], BF16, kind="ExternalInput")
    EGi = nc.dram_tensor("EGi", [NCH, 128, NH], F32, kind="ExternalInput")
    Zi = nc.dram_tensor("Zi", [32, 128, TPC], F32, kind="ExternalInput")
    R1i = nc.dram_tensor("R1i", [32, 128, TPC], F32, kind="ExternalInput")
    smask = nc.dram_tensor("smask", [1, 8], F32, kind="ExternalInput")
    mbig = nc.dram_tensor("mbig", [1, 8], F32, kind="ExternalInput")
    selv = nc.dram_tensor("selv", [1, 8], F32, kind="ExternalInput")
    Wop = nc.dram_tensor("Wop", [32, 128, NKT, 128], F32, kind="ExternalInput")
    nw = nc.dram_tensor("nw", [128, 32], F32, kind="ExternalInput")
    Wg = nc.dram_tensor("Wg", [NMI_G, 128, NKT, 128], I32, kind="ExternalInput")
    Wu = nc.dram_tensor("Wu", [NMI_G, 128, NKT, 128], I32, kind="ExternalInput")
    Wd = nc.dram_tensor("Wd", [32, 128, NMI_G, 128], I32, kind="ExternalInput")
    sgate = nc.dram_tensor("sgate", [128, NMI_G], F32, kind="ExternalInput")
    sup = nc.dram_tensor("sup", [128, NMI_G], F32, kind="ExternalInput")
    sdown = nc.dram_tensor("sdown", [128, 32], F32, kind="ExternalInput")
    cs = _np_consts()
    c_if_d = nc.inline_tensor(cs["c_if"], name="c_if")
    c_ocb_d = nc.inline_tensor(cs["c_ocb"], name="c_ocb")
    c_orf_d = nc.inline_tensor(cs["c_orf"], name="c_orf")

    out = nc.dram_tensor("out", [TPC, H], F32, kind="ExternalOutput")
    R2 = nc.dram_tensor("R2", [32, 128, TPC], F32)
    h2d = nc.dram_tensor("h2d", [NMI_G, 128, 512], BF16)
    wdbf = nc.dram_tensor("wdbf", [32, 128, INTER], BF16)

    C = Ctx(nc)
    _alloc_common(C, nc, final_path=True)
    C.sb_sgate = nc.alloc_sbuf_tensor("sb_sgate", [128, NMI_G], F32).ap()
    C.sb_sup = nc.alloc_sbuf_tensor("sb_sup", [128, NMI_G], F32).ap()
    C.sb_sdown = nc.alloc_sbuf_tensor("sb_sdown", [128, 32], F32).ap()
    C.sb_nw = nc.alloc_sbuf_tensor("sb_nw", [128, 32], F32).ap()
    C.c_if = nc.alloc_sbuf_tensor("sb_c_if", [128, 128], F32).ap()
    C.c_ocb = nc.alloc_sbuf_tensor("sb_c_ocb", [128, 1], BF16).ap()
    C.c_orf = nc.alloc_sbuf_tensor("sb_c_orf", [1, 128], F32).ap()
    yt = C.Bf32[:, 0:4096]
    yf = C.Bf32[:, 4096:8192]
    zc = C.Bf32[:, 8192:12288]
    gbuf = C.Bf32[:, 12288:16384]
    SIn = C.Bf32[:, 16384:20480]
    SIb = C.wut[0][:, :]
    EGsb = nc.alloc_sbuf_tensor("EGsb", [128, NCH * NH], F32).ap()
    CMsb = nc.alloc_sbuf_tensor("CMsb", [128, TPC], BF16).ap()
    Gc = nc.alloc_sbuf_tensor("Gc", [128, 8 * NH], F32).ap()
    gsel = nc.alloc_sbuf_tensor("gsel", [128, NH], F32).ap()
    coef = nc.alloc_sbuf_tensor("coef", [128, NH], F32).ap()
    mrows = nc.alloc_sbuf_tensor("mrows", [1, 24], F32).ap()
    mbc = nc.alloc_sbuf_tensor("mbc", [128, 24], F32).ap()

    PS = [nc.alloc_psum_tensor(f"ps{i}", [128, 512], F32) for i in range(8)]

    with nc.Block() as block:
        @block.gpsimd
        def _(g):
            for dst, src in ((C.sb_sgate, sgate), (C.sb_sup, sup),
                             (C.sb_sdown, sdown), (C.sb_nw, nw),
                             (C.c_if, c_if_d), (C.c_ocb, c_ocb_d),
                             (C.c_orf, c_orf_d), (CMsb, CMi)):
                C.dma(C.w, dst, src[:, :], "c")
            C.dma(C.w, mrows[0:1, 0:8], smask[:, :], "c")
            C.dma(C.w, mrows[0:1, 8:16], mbig[:, :], "c")
            C.dma(C.w, mrows[0:1, 16:24], selv[:, :], "c")
            t_eg = C.dma(C.w, EGsb.rearrange("p (a b) -> p a b", b=NH),
                         EGi[:, :, :].transpose([1, 0, 2]), "c")
            t_c = C.latest("c")

            C.wtok(C.t, t_c)
            tp = C.inc(nc.tensor.matmul(PS[4].ap()[:, 0:24], C.c_orf,
                                        mrows[0:1, 0:24], start=True,
                                        stop=True), "sP")
            C.wtok(C.v, tp)
            C.inc(C.v.tensor_copy(mbc, PS[4].ap()[:, 0:24]), "sV")

            lp = None
            for j in range(8):
                lp = C.dma(C.w, Gc[:, j * NH:(j + 1) * NH],
                           GB8[j, :, H:H + NH], "c")
            C.wtok(C.v, lp)
            for j in range(1, 8):
                C.v.tensor_tensor(Gc[:, j * NH:(j + 1) * NH],
                                  Gc[:, (j - 1) * NH:j * NH],
                                  Gc[:, j * NH:(j + 1) * NH], ALU.add)
            C.v.tensor_scalar(gsel, Gc[:, 0:NH], mbc[:, 16:17], None,
                              ALU.mult)
            for j in range(1, 8):
                C.v.scalar_tensor_tensor(gsel, Gc[:, j * NH:(j + 1) * NH],
                                         mbc[:, 16 + j:17 + j], gsel,
                                         ALU.mult, ALU.add)
            C.v.memset(SIn, 0.0)
            sj_tok = C.inc(C.v.engine_nop(), "sV")
            for j in range(8):
                C.v.tensor_tensor(coef, gsel, Gc[:, j * NH:(j + 1) * NH],
                                  ALU.subtract)
                tv = C.inc(C.v.tensor_scalar(coef, coef, mbc[:, j:j + 1],
                                             mbc[:, 8 + j:9 + j],
                                             ALU.mult, ALU.add), "sV")
                C.wtok(C.s, tv)
                ta = C.inc(C.s.activation(coef, coef, AF.Exp), "sA")
                C.wtok(C.w, sj_tok)
                t_sj = C.dma(C.w, yt, GB8[j, :, 0:H], "ly")
                C.wtok(C.v, ta, t_sj)
                for r in range(NH):
                    C.v.scalar_tensor_tensor(SIn[:, r * 64:(r + 1) * 64],
                                             yt[:, r * 64:(r + 1) * 64],
                                             coef[:, r:r + 1],
                                             SIn[:, r * 64:(r + 1) * 64],
                                             ALU.mult, ALU.add)
                sj_tok = C.inc(C.v.engine_nop(), "sV")
            sib_tok = C.inc(C.v.tensor_copy(SIb, SIn), "sV")
            if _UPTO == "sin":
                _final_wait(C)
                return

            C.wtok(C.v, t_eg)
            xo_tok = None
            for cch in range(NCH):
                C.wtok(C.w, C.latest("sV"), C.latest("sP"))
                C.wtok(C.t, C.latest("sV"))
                t_y = C.dma(C.w, yt, Yi[cch], "ly")
                t_z = C.dma(C.w, zc.rearrange("p (a b) -> p a b", b=128),
                            Zi[:, :, cch * Q:(cch + 1) * Q].transpose([1, 0, 2]),
                            "lz")
                cor = {}
                for gi in range(8):
                    if gi >= 2:
                        C.wtok(C.t, cor[gi - 2])
                    if gi < 2:
                        C.wtok(C.t, sib_tok)
                    i = nc.tensor.matmul(PS[gi % 2].ap()[:, :],
                                         CMsb[:, cch * Q:(cch + 1) * Q],
                                         SIb[:, gi * 512:(gi + 1) * 512],
                                         start=True, stop=True)
                    tp = C.inc(i, "sP")
                    C.wtok(C.v, tp)
                    if gi == 0:
                        C.wtok(C.v, t_y)
                    for h2 in range(8):
                        rr = gi * 8 + h2
                        C.v.scalar_tensor_tensor(
                            yt[:, rr * 64:(rr + 1) * 64],
                            PS[gi % 2].ap()[:, h2 * 64:(h2 + 1) * 64],
                            EGsb[:, cch * NH + rr:cch * NH + rr + 1],
                            yt[:, rr * 64:(rr + 1) * 64], ALU.mult, ALU.add)
                    cor[gi] = C.inc(C.v.engine_nop(), "sV")
                ytoks = []
                for p in range(32):
                    bkp = 2 + (p % 2)
                    if p >= 2:
                        C.wtok(C.t, ytoks[p - 2])
                    if p == 0:
                        C.wtok(C.t, cor[7])
                    i = nc.tensor.transpose(PS[bkp].ap()[:, 0:128],
                                            yt[:, p * 128:(p + 1) * 128],
                                            C.c_if)
                    tp = C.inc(i, "sP")
                    C.wtok(C.v, tp)
                    ytoks.append(C.inc(C.v.tensor_copy(
                        yf[:, p * 128:(p + 1) * 128],
                        PS[bkp].ap()[:, 0:128]), "sV"))
                C.wtok(C.s, ytoks[-1], t_z)
                ta = C.inc(C.s.activation(zc, zc, AF.Silu), "sA")
                C.wtok(C.v, ta)
                C.inc(C.v.tensor_tensor(gbuf, yf, zc, ALU.mult), "sV")
                ptoks = []
                for pt in range(32):
                    b = (pt % 2) * 1024
                    if pt >= 2:
                        C.wtok(C.v, ptoks[pt - 2])
                    C.v.tensor_copy(C.xb2[:, b:b + 128],
                                    gbuf[:, pt * 128:(pt + 1) * 128])
                    tv = C.inc(C.v.engine_nop(), "sV")
                    C.wtok(C.s, tv)
                    ts = C.inc(C.s.activation(C.xb2[:, b + 512:b + 640],
                                              C.xb2[:, b:b + 128],
                                              AF.Square), "sA")
                    C.wtok(C.t, ts)
                    i = nc.tensor.matmul(PS[6].ap()[0:1, 0:128], C.c_ocb,
                                         C.xb2[:, b + 512:b + 640],
                                         start=(pt == 0), stop=(pt == 31))
                    ptoks.append(C.inc(i, "sP"))
                rowa = C.row_m[0:1, 0:128]
                C.wtok(C.v, ptoks[-1])
                tv = C.inc(C.v.tensor_scalar(rowa, PS[6].ap()[0:1, 0:128],
                                             1.0 / H, EPS, ALU.mult, ALU.add),
                           "sV")
                C.wtok(C.s, tv)
                C.s.activation(rowa, rowa, AF.Ln)
                ta = C.inc(C.s.activation(rowa, rowa, AF.Exp, scale=-0.5),
                           "sA")
                C.wtok(C.t, ta)
                tp = C.inc(nc.tensor.matmul(PS[7].ap()[:, 0:128], C.c_orf,
                                            rowa, start=True, stop=True),
                           "sP")
                C.wtok(C.v, tp)
                for pt in range(32):
                    C.v.tensor_tensor(C.t1_512[:, 0:128],
                                      gbuf[:, pt * 128:(pt + 1) * 128],
                                      PS[7].ap()[:, 0:128], ALU.mult)
                    C.v.tensor_scalar(
                        C.A[:, pt * 1024 + cch * 128:pt * 1024 + cch * 128 + 128],
                        C.t1_512[:, 0:128], C.sb_nw[:, pt:pt + 1], None,
                        ALU.mult)
                xo_tok = C.inc(C.v.engine_nop(), "sV")
            if _UPTO == "m4":
                _final_wait(C)
                return

            wk = {}
            evo = {}
            last_store = None
            for mi in range(32 + 1):
                if mi < 32:
                    if mi >= 2:
                        C.wtok(C.g, evo[(mi - 2, "pe")])
                    wk[mi] = C.dma(C.g, C.wgt[mi % 2][:, :],
                                   Wop[mi].rearrange("p a b -> p (a b)"),
                                   f"wg{mi % 2}")
                if mi >= 1:
                    j = mi - 1
                    C.wtok(C.t, wk[j])
                    if j < 2:
                        C.wtok(C.t, xo_tok)
                    else:
                        C.wtok(C.t, evo[(j - 2, "dve")])
                    b0 = 4 + (j % 2) * 2
                    for kt in range(NKT):
                        lw = C.wgt[j % 2][:, kt * 128:(kt + 1) * 128]
                        st = (kt == 0)
                        sp = (kt == NKT - 1)
                        nc.tensor.matmul(PS[b0][:, :], lw,
                                         C.A[:, kt * 1024:kt * 1024 + 512],
                                         start=st, stop=sp)
                        i = nc.tensor.matmul(PS[b0 + 1][:, :], lw,
                                             C.A[:, kt * 1024 + 512:(kt + 1) * 1024],
                                             start=st, stop=sp)
                    evo[(j, "pe")] = C.inc(i, "sP")
                    tv = None
                    for hf in (0, 1):
                        t_rx = C.dma(C.w, C.rx,
                                     R1i[j, :, hf * 512:(hf + 1) * 512], "rx")
                        C.wtok(C.v, evo[(j, "pe")], t_rx, last_store)
                        tv = C.inc(C.v.tensor_tensor(C.ost, PS[b0 + hf][:, :],
                                                     C.rx, ALU.add), "sV")
                        C.wtok(C.w, tv)
                        last_store = C.dma(C.w,
                                           R2[j, :, hf * 512:(hf + 1) * 512],
                                           C.ost, "os")
                    evo[(j, "dve")] = tv
            if _UPTO == "outproj":
                _final_wait(C)
                return

            for hf in (0, 1):
                nrm_tok = _emit_norm_half(C, R2, hf, "ln", PS)
            _emit_ffn(C, R2, None, PS, Wg, Wu, Wd, wdbf, h2d, final_out=out,
                      a_ready=nrm_tok)
            _final_wait(C)
    return nc



def kernel(**inputs):
    _ = inputs.get("L0_attn_residual")
    hp = _prep_host(inputs)

    shared1 = {k: hp[k] for k in
               ("Wg", "Wu", "Wd", "Wip", "sgate", "sup", "sdown", "mup",
                "lnw")}
    nc1 = _build_L1()
    maps1 = [dict(shared1, x_in=hp["xs"][c]) for c in range(8)]
    res1 = run_bass_kernel_spmd(nc1, maps1, core_ids=list(range(8)))
    r1 = res1.results

    nc2 = _build_L2()
    maps2 = []
    for c in range(8):
        xbc_s = np.zeros((NPT_C, 128, TPC + 3), np.float32)
        xbc_s[:, :, 3:] = r1[c]["XBCo"]
        if c % 4 != 0:
            xbc_s[:, :, 0:3] = r1[c - 1]["XBCo"][:, :, TPC - 3:TPC]
        maps2.append({
            "xbc": xbc_s, "dt_in": r1[c]["DTo"], "alog": hp["alog"],
            "dtb_in": hp["dtb"], "Drow": hp["Drow"], "convw": hp["convw"],
        })
    res2 = run_bass_kernel_spmd(nc2, maps2, core_ids=list(range(8)))
    r2 = res2.results

    GB8 = np.ascontiguousarray(np.stack([r2[c]["GBo"] for c in range(8)]))
    nc3 = _build_L3()
    shared3 = {k: hp[k] for k in ("Wg", "Wu", "Wd", "Wop", "sgate", "sup",
                                  "sdown", "nw")}
    maps3 = []
    for c in range(8):
        sm = np.zeros((1, 8), np.float32)
        sv = np.zeros((1, 8), np.float32)
        for j in range(8):
            if j < c and j // 4 == c // 4:
                sm[0, j] = 1.0
        if c % 4 != 0:
            sv[0, c - 1] = 1.0
        mb = (-30000.0) * (1.0 - sm)
        maps3.append(dict(
            shared3, GB8=GB8, Yi=r2[c]["Yo"], CMi=r2[c]["CMo"],
            EGi=r2[c]["EGo"], Zi=r1[c]["Zo"], R1i=r1[c]["R1o"],
            smask=sm, mbig=mb, selv=sv))
    res3 = run_bass_kernel_spmd(nc3, maps3, core_ids=list(range(8)))
    outs = [res3.results[c]["out"] for c in range(8)]
    return np.concatenate(outs, axis=0).reshape(2, 4096, H).astype(np.float32)


# version 04 (final)
# speedup vs baseline: 1.0550x; 1.0550x over previous
import numpy as np
import ml_dtypes
import concourse.bass as bass
import concourse.mybir as mybir
from concourse.bass_utils import run_bass_kernel_spmd

F32 = mybir.dt.float32
BF16 = mybir.dt.bfloat16
I32 = mybir.dt.int32
AF = mybir.ActivationFunctionType
ALU = mybir.AluOpType
BF = ml_dtypes.bfloat16

H = 4096
PROJ = 8512
PROJP = 8576
CONV = 4352
INTER = 11008
NH = 64
EPS = 1e-5
TPC = 1024
NCH = 8
Q = 128
NKT = 32
NMI_G = 86
NMI_P = 67
NPT_C = 34

_UPTO = None
_DETECT = True


_CHAIN_OPS = frozenset((
    "tensor_copy", "tensor_tensor", "tensor_scalar", "tensor_single_scalar",
    "tensor_scalar_add", "tensor_scalar_sub", "tensor_scalar_mul",
    "tensor_scalar_max", "tensor_scalar_min", "tensor_relu", "tensor_add",
    "tensor_sub", "tensor_mul", "tensor_max", "scalar_tensor_tensor",
    "tensor_tensor_scan", "tensor_reduce", "reduce_max", "reduce_sum",
    "memset", "memzero", "engine_nop", "select", "transpose", "reciprocal",
    "activation", "square", "sqrt", "mul", "copy", "add", "sign",
))


class _ChainEng:

    def __init__(self, C, eng, semname):
        self._C = C
        self._eng = eng
        self._sem = semname

    def __getattr__(self, name):
        f = getattr(self._eng, name)
        if name not in _CHAIN_OPS:
            return f
        C = self._C

        def wrapped(*a, **k):
            s, v = C.latest(self._sem)
            if v > 0:
                self._eng.wait_ge(C.sem[s], v)
            inst = f(*a, **k)
            inst.then_inc(C.sem[s], 1)
            C.cnt[s] += 1
            return inst

        return wrapped


class Ctx:

    def __init__(self, nc):
        self.nc = nc
        self.sem = {}
        self.cnt = {}
        self.g = nc.gpsimd
        self.w = nc.sync
        self.t = nc.tensor
        self.v = _ChainEng(self, nc.vector, "sV")
        self.s = _ChainEng(self, nc.scalar, "sA")
        self.p = _ChainEng(self, nc.gpsimd, "sG")
        self._chain = {"sV": self.v, "sA": self.s, "sG": self.p}

    def _get(self, s):
        if s not in self.sem:
            self.sem[s] = self.nc.alloc_semaphore(s)
            self.cnt[s] = 0
        return self.sem[s]

    def inc(self, inst, s, n=1):
        if s in ("sV", "sA", "sG"):
            return (s, self.cnt[s])
        h = self._get(s)
        inst.then_inc(h, n)
        self.cnt[s] += n
        return (s, self.cnt[s])

    def latest(self, s):
        self._get(s)
        return (s, self.cnt[s])

    def wtok(self, eng, *toks):
        for t in toks:
            if t is None:
                continue
            s, v = t
            if v > 0:
                eng.wait_ge(self.sem[s], v)

    def dma(self, eng, out, in_, s):
        i = eng.dma_start(out=out, in_=in_)
        return self.inc(i, s, 16)



def _pack_lhsT(w_km):
    K, M = w_km.shape
    return np.ascontiguousarray(
        w_km.reshape(K // 128, 128, M // 128, 128).transpose(2, 1, 0, 3))


def _pack_cols(v):
    return np.ascontiguousarray(v.reshape(-1, 128).T)


def _prep_host(inputs):
    f = lambda a: np.ascontiguousarray(np.asarray(a))
    hp = {}
    wg = f(inputs["L0_ffn_w_gate"]).astype(np.int32)
    wu = f(inputs["L0_ffn_w_up"]).astype(np.int32)
    wd = f(inputs["L0_ffn_w_down"]).astype(np.int32)
    hp["Wg"] = _pack_lhsT(wg.T)
    hp["Wu"] = _pack_lhsT(wu.T)
    hp["Wd"] = _pack_lhsT(wd.T)
    wip = f(inputs["L0_attn_in_proj_w"]).astype(np.float32)
    wip_p = np.zeros((H, PROJP), np.float32)
    wip_p[:, :PROJ] = wip
    hp["Wip"] = _pack_lhsT(wip_p)
    hp["Wop"] = _pack_lhsT(f(inputs["L0_attn_out_proj_w"]).astype(np.float32))
    hp["sgate"] = _pack_cols(f(inputs["L0_ffn_s_gate"]).astype(np.float32))
    hp["sup"] = _pack_cols(f(inputs["L0_ffn_s_up"]).astype(np.float32))
    hp["sdown"] = _pack_cols(f(inputs["L0_ffn_s_down"]).astype(np.float32))
    mup_p = np.zeros((PROJP,), np.float32)
    mup_p[:PROJ] = f(inputs["L0_attn_mup_vector"]).astype(np.float32)
    hp["mup"] = _pack_cols(mup_p)
    hp["lnw"] = _pack_cols(f(inputs["L0_attn_ln_w"]).astype(np.float32))
    hp["nw"] = _pack_cols(f(inputs["L0_attn_norm_w"]).astype(np.float32))
    cw = f(inputs["L0_attn_conv_w"]).astype(np.float32)
    hp["convw"] = np.ascontiguousarray(
        cw.reshape(NPT_C, 128, 4).transpose(1, 0, 2).reshape(128, NPT_C * 4))
    hp["alog"] = f(inputs["L0_attn_A_log"]).astype(np.float32).reshape(NH, 1)
    hp["dtb"] = f(inputs["L0_attn_dt_bias"]).astype(np.float32).reshape(NH, 1)
    hp["Drow"] = f(inputs["L0_attn_D"]).astype(np.float32).reshape(1, NH)
    x = f(inputs["L0_attn_hidden_states"]).astype(np.float32).reshape(8192, H)
    hp["xs"] = [np.ascontiguousarray(x[c * TPC:(c + 1) * TPC]) for c in range(8)]
    return hp


def _np_consts():
    return {
        "c_if": np.eye(128, dtype=np.float32),
        "c_ib": np.eye(128, dtype=BF),
        "c_ocb": np.ones((128, 1), dtype=BF),
        "c_orf": np.ones((1, 128), dtype=np.float32),
        "c_um": np.triu(np.ones((128, 128), np.float32)),
    }



def _emit_norm_half(C, src_dram, hf, kind, PS, lnw_sb=None):
    nc = C.nc
    C.wtok(C.w, C.latest("sV"), C.latest("sP"), C.latest("os"))
    t_ld = C.dma(C.w, C.nrm.rearrange("p (a b) -> p a b", b=512),
                 src_dram[:, :, hf * 512:(hf + 1) * 512].transpose([1, 0, 2]),
                 "ln")
    C.wtok(C.v, t_ld)
    pe_toks = []
    for pt in range(NKT):
        b = (pt % 2) * 1024
        xcols = C.nrm[:, pt * 512:(pt + 1) * 512]
        if pt >= 2:
            C.wtok(C.v, pe_toks[pt - 2])
        tv = C.inc(C.v.tensor_copy(C.xb2[:, b:b + 512], xcols), "sV")
        C.wtok(C.s, tv)
        ts = C.inc(C.s.activation(C.xb2[:, b + 512:b + 1024],
                                  C.xb2[:, b:b + 512], AF.Square), "sA")
        C.wtok(C.t, ts)
        nc.tensor.matmul(PS[6][0:1, :], C.c_ocb, C.xb2[:, b:b + 512],
                         start=(pt == 0), stop=(pt == NKT - 1))
        i = nc.tensor.matmul(PS[5][0:1, :], C.c_ocb,
                             C.xb2[:, b + 512:b + 1024],
                             start=(pt == 0), stop=(pt == NKT - 1))
        pe_toks.append(C.inc(i, "sP"))
    m = C.row_m
    r = C.row_r
    mr = C.row_mr
    tmp = C.row_tmp
    C.wtok(C.v, pe_toks[-1])
    C.v.tensor_scalar(m, PS[6][0:1, :], 1.0 / H, None, ALU.mult)
    C.v.tensor_scalar(tmp, PS[5][0:1, :], 1.0 / H, None, ALU.mult)
    C.v.tensor_tensor(r, m, m, ALU.mult)
    C.v.tensor_tensor(tmp, tmp, r, ALU.subtract)
    if kind == "ln":
        tv = C.inc(C.v.tensor_scalar(tmp, tmp, 1.0, EPS, ALU.mult, ALU.add),
                   "sV")
    else:
        tv = C.inc(C.v.tensor_scalar(tmp, tmp, 1.0 + EPS, EPS * EPS,
                                     ALU.mult, ALU.add), "sV")
    C.wtok(C.s, tv)
    C.s.activation(tmp, tmp, AF.Ln)
    ta = C.inc(C.s.activation(r, tmp, AF.Exp, scale=-0.5), "sA")
    C.wtok(C.v, ta)
    tv = C.inc(C.v.tensor_tensor(mr, m, r, ALU.mult), "sV")
    C.wtok(C.t, tv)
    nc.tensor.matmul(PS[7][:, :], C.c_orf, r, start=True, stop=True)
    tp = C.inc(nc.tensor.matmul(PS[5][:, :], C.c_orf, mr,
                                start=True, stop=True), "sP")
    C.wtok(C.v, tp)
    last = None
    for pt in range(NKT):
        xcols = C.nrm[:, pt * 512:(pt + 1) * 512]
        dst = C.A[:, pt * 1024 + hf * 512: pt * 1024 + hf * 512 + 512]
        C.v.tensor_tensor(C.t1_512, xcols, PS[7][:, :], ALU.mult)
        if kind == "ln":
            last = C.inc(C.v.tensor_tensor(dst, C.t1_512, PS[5][:, :],
                                           ALU.subtract), "sV")
        else:
            C.v.tensor_tensor(C.t1_512, C.t1_512, PS[5][:, :], ALU.subtract)
            last = C.inc(C.v.tensor_scalar(dst, C.t1_512,
                                           lnw_sb[:, pt:pt + 1], None,
                                           ALU.mult), "sV")
    return last


def _emit_ffn(C, src_dram, dst_dram, PS, Wg, Wu, Wd, wdbf, h2d, final_out=None,
              a_ready=None):
    nc = C.nc
    ev = {}
    wtoks = {}
    for mi in range(NMI_G + 1):
        if mi < NMI_G:
            if mi < 2:
                C.wtok(C.g, C.latest("sP"))
            else:
                C.wtok(C.g, ev[(mi - 2, "pe")])
            if wdbf is not None and mi % 2 == 0 and mi // 2 < 32:
                j2 = mi // 2
                C.dma(C.g, wdbf[j2], Wd[j2].rearrange("p a b -> p (a b)"),
                      "cw")
            wtoks[mi] = (
                C.dma(C.g, C.wgt[mi % 2][:, :],
                      Wg[mi].rearrange("p a b -> p (a b)"), f"wg{mi % 2}"),
                C.dma(C.g, C.wut[mi % 2][:, :],
                      Wu[mi].rearrange("p a b -> p (a b)"), f"wu{mi % 2}"))
        if mi >= 1:
            j = mi - 1
            C.wtok(C.t, wtoks[j][0], wtoks[j][1])
            if j < 2:
                C.wtok(C.t, a_ready)
            else:
                C.wtok(C.t, ev[(j - 2, "dve")])
            bk = (j % 2) * 4
            for kt in range(NKT):
                lw_g = C.wgt[j % 2][:, kt * 128:(kt + 1) * 128]
                lw_u = C.wut[j % 2][:, kt * 128:(kt + 1) * 128]
                st = (kt == 0)
                sp = (kt == NKT - 1)
                nc.tensor.matmul(PS[bk + 0][:, :], lw_g,
                                 C.A[:, kt * 1024:kt * 1024 + 512],
                                 start=st, stop=sp)
                nc.tensor.matmul(PS[bk + 1][:, :], lw_g,
                                 C.A[:, kt * 1024 + 512:kt * 1024 + 1024],
                                 start=st, stop=sp)
                nc.tensor.matmul(PS[bk + 2][:, :], lw_u,
                                 C.A[:, kt * 1024:kt * 1024 + 512],
                                 start=st, stop=sp)
                i = nc.tensor.matmul(PS[bk + 3][:, :], lw_u,
                                     C.A[:, kt * 1024 + 512:kt * 1024 + 1024],
                                     start=st, stop=sp)
            ev[(j, "pe")] = C.inc(i, "sP")
            C.wtok(C.s, ev[(j, "pe")])
            if j >= 1:
                C.wtok(C.s, ev[(j - 1, "dve")])
            C.s.activation(C.tg1, PS[bk + 0][:, :], AF.Silu,
                           scale=C.sb_sgate[:, j:j + 1])
            ta = C.inc(C.s.activation(C.tg2, PS[bk + 1][:, :], AF.Silu,
                                      scale=C.sb_sgate[:, j:j + 1]), "sA")
            C.wtok(C.v, ta)
            if j >= 1:
                C.wtok(C.v, ev[(j - 1, "st")])
            C.v.scalar_tensor_tensor(C.B[:, j * 512:(j + 1) * 512],
                                     PS[bk + 2][:, :], C.sb_sup[:, j:j + 1],
                                     C.tg1, ALU.mult, ALU.mult)
            tv = C.inc(C.v.scalar_tensor_tensor(C.h2t, PS[bk + 3][:, :],
                                                C.sb_sup[:, j:j + 1], C.tg2,
                                                ALU.mult, ALU.mult), "sV")
            ev[(j, "dve")] = tv
            C.wtok(C.w, tv)
            ev[(j, "st")] = C.dma(C.w, h2d[j], C.h2t, "h2s")

    if _UPTO == "gateup":
        C.wtok(C.g, ev[(NMI_G - 1, "dve")], C.latest("h2s"))
        for j in range(NMI_G):
            d = (C.dbg_dumps[0][j] if j < 32 else
                 C.dbg_dumps[1][j - 32] if j < 66 else
                 C.dbg_dumps[2][j - 66])
            C.dma(C.g, d[:, 0:512], C.B[:, j * 512:(j + 1) * 512], "dump")
            C.dma(C.g, d[:, 512:1024], h2d[j], "dump")
        return None
    if _UPTO == "down0":
        halves = (0,)
    else:
        halves = (0, 1)

    last_store = None
    last_tr = None
    prev_half_dve = ev[(NMI_G - 1, "dve")]
    prev_half_pe = ev[(NMI_G - 1, "pe")]
    first = True
    for hf in halves:
        t_h2 = None
        if hf == 1:
            C.wtok(C.w, C.latest("sP"), C.latest("h2s"))
            t_h2 = C.dma(C.w, C.B.ap().rearrange("p (a b) -> p a b", b=512),
                         h2d[:, :, :].transpose([1, 0, 2]), "h2l")
        wdk = {}
        evd = {}
        for o in range(32 + 1):
            if o < 32:
                if o < 2:
                    C.wtok(C.w, prev_half_pe)
                    if first and wdbf is not None:
                        C.wtok(C.w, C.latest("cw"))
                        first = False
                else:
                    C.wtok(C.w, evd[(o - 2, "pe")])
                wdk[o] = C.dma(C.w, C.wdt[o % 2], wdbf[o], f"wdl{o % 2}")
            if o >= 1:
                j = o - 1
                C.wtok(C.t, wdk[j], t_h2)
                if j < 2:
                    C.wtok(C.t, prev_half_dve)
                else:
                    C.wtok(C.t, evd[(j - 2, "dve")])
                bk = j % 2
                for k in range(NMI_G):
                    i = nc.tensor.matmul(PS[bk][:, :],
                                         C.wdt[j % 2][:, k * 128:(k + 1) * 128],
                                         C.B[:, k * 512:(k + 1) * 512],
                                         start=(k == 0), stop=(k == NMI_G - 1))
                evd[(j, "pe")] = C.inc(i, "sP")
                t_rx = C.dma(C.w, C.rx, src_dram[j, :, hf * 512:(hf + 1) * 512],
                             "rx")
                C.wtok(C.v, evd[(j, "pe")], t_rx, last_store, last_tr)
                C.v.tensor_scalar(C.t1_512, PS[bk][:, :],
                                  C.sb_sdown[:, j:j + 1], 0.5,
                                  ALU.mult, ALU.mult)
                tv = C.inc(C.v.tensor_tensor(C.ost, C.t1_512, C.rx, ALU.add),
                           "sV")
                evd[(j, "dve")] = tv
                if final_out is None:
                    C.wtok(C.w, tv)
                    last_store = C.dma(C.w,
                                       dst_dram[j, :, hf * 512:(hf + 1) * 512],
                                       C.ost, "os")
                else:
                    C.wtok(C.t, tv)
                    for tq in range(4):
                        bkp = 2 + (tq % 2)
                        if tq >= 2:
                            C.wtok(C.t, evd[(j, "tr%d" % (tq - 2))])
                        i = nc.tensor.transpose(
                            PS[bkp].ap()[:, 0:128],
                            C.ost[:, tq * 128:(tq + 1) * 128], C.c_if)
                        last_tr = C.inc(i, "sP")
                        C.wtok(C.v, last_tr, last_store)
                        tve = C.inc(C.v.tensor_copy(
                            C.tstg[:, tq * 128:(tq + 1) * 128],
                            PS[bkp].ap()[:, 0:128]), "sV")
                        evd[(j, "tr%d" % tq)] = tve
                        C.wtok(C.w, tve)
                        r0 = hf * 512 + tq * 128
                        last_store = C.dma(
                            C.w, final_out[r0:r0 + 128, j * 128:(j + 1) * 128],
                            C.tstg[:, tq * 128:(tq + 1) * 128], "os")
        prev_half_dve = evd[(31, "dve")]
        prev_half_pe = evd[(31, "pe")]
    return last_store


def _alloc_common(C, nc, final_path=False):
    C.A = nc.alloc_sbuf_tensor("A", [128, 32 * 1024], BF16)
    C.Abf = C.A.ap()
    C.B = nc.alloc_sbuf_tensor("Bh", [128, NMI_G * 512], BF16)
    C.Bf32 = C.B.ap().bitcast(F32)
    C.nrm = C.Bf32[:, 0:NKT * 512]
    C.wdt = [C.Abf[:, 0:INTER], C.Abf[:, INTER:2 * INTER]]
    C.wgt = [nc.alloc_sbuf_tensor(f"wgt{i}", [128, H], BF16) for i in range(2)]
    C.wut = [nc.alloc_sbuf_tensor(f"wut{i}", [128, H], BF16) for i in range(2)]
    C.xb2 = nc.alloc_sbuf_tensor("xb2", [128, 2048], BF16).ap()
    C.t1_512 = nc.alloc_sbuf_tensor("t1_512", [128, 512], F32).ap()
    C.rx = nc.alloc_sbuf_tensor("rx", [128, 512], F32).ap()
    C.ost = nc.alloc_sbuf_tensor("ost", [128, 512], F32).ap()
    C.tg1 = nc.alloc_sbuf_tensor("tg1", [128, 512], F32).ap()
    C.tg2 = nc.alloc_sbuf_tensor("tg2", [128, 512], F32).ap()
    C.h2t = nc.alloc_sbuf_tensor("h2t", [128, 512], BF16).ap()
    if final_path:
        C.tstg = nc.alloc_sbuf_tensor("tstg", [128, 512], F32).ap()
    C.row_m = C.tg1[0:1, :]
    C.row_tmp = C.tg2[0:1, :]
    C.row_r = C.rx[0:1, :]
    C.row_mr = C.ost[0:1, :]


def _final_wait(C):
    for s in sorted(C.cnt.keys()):
        C.wtok(C.g, C.latest(s))



def _build_L1():
    nc = bass.Bass("TRN2", target_bir_lowering=False, detect_race_conditions=_DETECT)
    x_in = nc.dram_tensor("x_in", [TPC, H], F32, kind="ExternalInput")
    Wg = nc.dram_tensor("Wg", [NMI_G, 128, NKT, 128], I32, kind="ExternalInput")
    Wu = nc.dram_tensor("Wu", [NMI_G, 128, NKT, 128], I32, kind="ExternalInput")
    Wd = nc.dram_tensor("Wd", [32, 128, NMI_G, 128], I32, kind="ExternalInput")
    Wip = nc.dram_tensor("Wip", [NMI_P, 128, NKT, 128], F32, kind="ExternalInput")
    sgate = nc.dram_tensor("sgate", [128, NMI_G], F32, kind="ExternalInput")
    sup = nc.dram_tensor("sup", [128, NMI_G], F32, kind="ExternalInput")
    sdown = nc.dram_tensor("sdown", [128, 32], F32, kind="ExternalInput")
    mup = nc.dram_tensor("mup", [128, NMI_P], F32, kind="ExternalInput")
    lnw = nc.dram_tensor("lnw", [128, 32], F32, kind="ExternalInput")
    cs = _np_consts()
    c_if_d = nc.inline_tensor(cs["c_if"], name="c_if")
    c_ocb_d = nc.inline_tensor(cs["c_ocb"], name="c_ocb")
    c_orf_d = nc.inline_tensor(cs["c_orf"], name="c_orf")

    R1o = nc.dram_tensor("R1o", [32, 128, TPC], F32, kind="ExternalOutput")
    Zo = nc.dram_tensor("Zo", [32, 128, TPC], F32, kind="ExternalOutput")
    XBCo = nc.dram_tensor("XBCo", [NPT_C, 128, TPC], F32, kind="ExternalOutput")
    DTo = nc.dram_tensor("DTo", [NH, TPC], F32, kind="ExternalOutput")

    Xf = nc.dram_tensor("Xf", [32, 128, TPC], F32)
    h2d = nc.dram_tensor("h2d", [NMI_G, 128, 512], BF16)
    wdbf = nc.dram_tensor("wdbf", [32, 128, INTER], BF16)

    C = Ctx(nc)
    _alloc_common(C, nc)
    C.sb_sgate = nc.alloc_sbuf_tensor("sb_sgate", [128, NMI_G], F32).ap()
    C.sb_sup = nc.alloc_sbuf_tensor("sb_sup", [128, NMI_G], F32).ap()
    C.sb_sdown = nc.alloc_sbuf_tensor("sb_sdown", [128, 32], F32).ap()
    C.sb_mup = nc.alloc_sbuf_tensor("sb_mup", [128, NMI_P], F32).ap()
    C.sb_lnw = nc.alloc_sbuf_tensor("sb_lnw", [128, 32], F32).ap()
    C.c_if = nc.alloc_sbuf_tensor("sb_c_if", [128, 128], F32).ap()
    C.c_ocb = nc.alloc_sbuf_tensor("sb_c_ocb", [128, 1], BF16).ap()
    C.c_orf = nc.alloc_sbuf_tensor("sb_c_orf", [1, 128], F32).ap()
    ipst = nc.alloc_sbuf_tensor("ipst", [128, 1024], F32).ap()
    dtst = ipst[0:64, :]

    PS = [nc.alloc_psum_tensor(f"ps{i}", [128, 512], F32) for i in range(8)]

    with nc.Block() as block:
        @block.gpsimd
        def _(g):
            t_c = None
            for dst, src in ((C.sb_sgate, sgate), (C.sb_sup, sup),
                             (C.sb_sdown, sdown), (C.sb_mup, mup),
                             (C.sb_lnw, lnw), (C.c_if, c_if_d),
                             (C.c_ocb, c_ocb_d), (C.c_orf, c_orf_d)):
                t_c = C.dma(C.w, dst, src[:, :], "c")
            if _UPTO == "consts":
                _final_wait(C)
                return

            xt = [C.Bf32[:, 0:4096], C.Bf32[:, 4096:8192]]
            stg = [C.Bf32[:, 8192:12288], C.Bf32[:, 12288:16384]]
            ld = {}
            tr_tok = {}
            st_tok = {}
            for tc in range(8):
                par = tc % 2
                if tc >= 2:
                    C.wtok(C.w, tr_tok[tc - 2])
                ld[tc] = C.dma(C.w, xt[par], x_in[tc * 128:(tc + 1) * 128, :],
                               f"lx{par}")
                C.wtok(C.t, ld[tc], t_c)
                if tc >= 2:
                    C.wtok(C.t, st_tok[tc - 2])
                ptoks = []
                for p in range(NKT):
                    bkp = p % 4
                    if p >= 4:
                        C.wtok(C.t, ptoks[p - 4])
                    i = nc.tensor.transpose(PS[bkp].ap()[:, 0:128],
                                            xt[par][:, p * 128:(p + 1) * 128],
                                            C.c_if)
                    tp = C.inc(i, "sP")
                    C.wtok(C.v, tp)
                    ptoks.append(C.inc(C.v.tensor_copy(
                        stg[par][:, p * 128:(p + 1) * 128],
                        PS[bkp].ap()[:, 0:128]), "sV"))
                tr_tok[tc] = C.latest("sP")
                C.wtok(C.w, ptoks[-1])
                st_tok[tc] = C.dma(
                    C.w, Xf[:, :, tc * 128:(tc + 1) * 128].transpose([1, 0, 2]),
                    stg[par].rearrange("p (a b) -> p a b", b=128), f"sx{par}")
            C.wtok(C.w, st_tok[6], st_tok[7])
            if _UPTO == "entry":
                C.wtok(C.g, st_tok[6], st_tok[7])
                for kt in range(32):
                    C.dma(C.g, R1o[kt], Xf[kt], "dump")
                _final_wait(C)
                return

            for hf in (0, 1):
                nrm_tok = _emit_norm_half(C, Xf, hf, "ln", PS)
            if _UPTO == "norm":
                C.wtok(C.g, nrm_tok)
                for kt in range(32):
                    C.dma(C.g, R1o[kt],
                          C.A.ap()[:, kt * 1024:(kt + 1) * 1024], "dump")
                _final_wait(C)
                return
            C.dbg_dumps = (R1o, XBCo, Zo)
            _emit_ffn(C, Xf, R1o, PS, Wg, Wu, Wd, wdbf, h2d, a_ready=nrm_tok)
            if _UPTO in ("ffn", "gateup", "down0"):
                _final_wait(C)
                return

            for hf in (0, 1):
                nrm_tok = _emit_norm_half(C, R1o, hf, "ln_rms", PS,
                                          lnw_sb=C.sb_lnw)
            if _UPTO == "mixnorm":
                _final_wait(C)
                return

            wk = {}
            evp = {}
            last_store = None
            for mi in range(NMI_P + 1):
                if mi < NMI_P:
                    if mi < 2:
                        C.wtok(C.g, C.latest("sP"))
                    else:
                        C.wtok(C.g, evp[(mi - 2, "pe")])
                    wk[mi] = C.dma(C.g, C.wgt[mi % 2][:, :],
                                   Wip[mi].rearrange("p a b -> p (a b)"),
                                   f"wg{mi % 2}")
                if mi >= 1:
                    j = mi - 1
                    C.wtok(C.t, wk[j])
                    if j < 2:
                        C.wtok(C.t, nrm_tok)
                    else:
                        C.wtok(C.t, evp[(j - 2, "dve")])
                    b0 = (j % 2) * 2
                    for kt in range(NKT):
                        lw = C.wgt[j % 2][:, kt * 128:(kt + 1) * 128]
                        st = (kt == 0)
                        sp = (kt == NKT - 1)
                        nc.tensor.matmul(PS[b0][:, :], lw,
                                         C.A[:, kt * 1024:kt * 1024 + 512],
                                         start=st, stop=sp)
                        i = nc.tensor.matmul(PS[b0 + 1][:, :], lw,
                                             C.A[:, kt * 1024 + 512:(kt + 1) * 1024],
                                             start=st, stop=sp)
                    evp[(j, "pe")] = C.inc(i, "sP")
                    C.wtok(C.v, evp[(j, "pe")], last_store)
                    if j < 66:
                        C.v.tensor_scalar(ipst[:, 0:512], PS[b0][:, :],
                                          C.sb_mup[:, j:j + 1], None, ALU.mult)
                        tv = C.inc(C.v.tensor_scalar(ipst[:, 512:1024],
                                                     PS[b0 + 1][:, :],
                                                     C.sb_mup[:, j:j + 1],
                                                     None, ALU.mult), "sV")
                        evp[(j, "dve")] = tv
                        C.wtok(C.w, tv)
                        dst = Zo[j] if j < 32 else XBCo[j - 32]
                        last_store = C.dma(C.w, dst, ipst, "os")
                    else:
                        C.v.tensor_scalar(dtst[:, 0:512], PS[b0][0:64, :],
                                          C.sb_mup[0:64, j:j + 1], None,
                                          ALU.mult)
                        tv = C.inc(C.v.tensor_scalar(dtst[:, 512:1024],
                                                     PS[b0 + 1][0:64, :],
                                                     C.sb_mup[0:64, j:j + 1],
                                                     None, ALU.mult), "sV")
                        evp[(j, "dve")] = tv
                        C.wtok(C.w, tv)
                        last_store = C.dma(C.w, DTo[:, :], dtst, "os")
            _final_wait(C)
    return nc



def _build_L2():
    nc = bass.Bass("TRN2", target_bir_lowering=False, detect_race_conditions=_DETECT)
    xbc = nc.dram_tensor("xbc", [NPT_C, 128, TPC + 3], F32, kind="ExternalInput")
    dt_in = nc.dram_tensor("dt_in", [NH, TPC], F32, kind="ExternalInput")
    alog = nc.dram_tensor("alog", [NH, 1], F32, kind="ExternalInput")
    dtb_in = nc.dram_tensor("dtb_in", [NH, 1], F32, kind="ExternalInput")
    Drow = nc.dram_tensor("Drow", [1, NH], F32, kind="ExternalInput")
    convw = nc.dram_tensor("convw", [128, NPT_C * 4], F32, kind="ExternalInput")
    cs = _np_consts()
    cst = {k: nc.inline_tensor(v, name=k) for k, v in cs.items()}

    Yo = nc.dram_tensor("Yo", [NCH, 128, H], F32, kind="ExternalOutput")
    CMo = nc.dram_tensor("CMo", [128, TPC], BF16, kind="ExternalOutput")
    EGo = nc.dram_tensor("EGo", [NCH, 128, NH], F32, kind="ExternalOutput")
    GBo = nc.dram_tensor("GBo", [128, H + NH], F32, kind="ExternalOutput")

    arow_d = nc.dram_tensor("arow_d", [NH, Q], F32)

    C = Ctx(nc)
    sbt = nc.alloc_sbuf_tensor
    xin = sbt("xin", [128, NPT_C * 131], F32).ap()
    cv = sbt("cv", [128, NPT_C * 128], F32).ap()
    xs = sbt("xs", [128, 32 * 128], F32).ap()
    Bm = sbt("Bm", [128, 128], BF16).ap()
    BmT = sbt("BmT", [128, 128], BF16).ap()
    CMall = sbt("CMall", [128, TPC], BF16).ap()
    G2m = sbt("G2m", [128, 128], BF16).ap()
    xsT = sbt("xsT", [128, H], BF16).ap()
    X3 = sbt("X3", [128, H], BF16).ap()
    SR = sbt("SR", [128, H], F32).ap()
    SRb = sbt("SRb", [128, H], BF16).ap()
    yT = sbt("yT", [128, H], F32).ap()
    wr = sbt("wr", [128, 4 * 128], BF16).ap()
    Et = sbt("Et", [128, 4 * 128 + 256], F32).ap()
    hs = sbt("hs", [NH, 8 * TPC], F32).ap()
    sp_ = hs[:, 0:TPC]
    gcum = hs[:, TPC:2 * TPC]
    arel = hs[:, 2 * TPC:3 * TPC]
    erow = hs[:, 3 * TPC:4 * TPC]
    eg = hs[:, 4 * TPC:5 * TPC]
    w3 = hs[:, 5 * TPC:6 * TPC]
    htmp = hs[:, 6 * TPC:7 * TPC]
    zeros = hs[:, 7 * TPC:8 * TPC]
    dtsb = sbt("dtsb", [NH, TPC], F32).ap()
    acol = sbt("acol", [NH, 2], F32).ap()
    tcol = sbt("tcol", [128, 8 * NH], F32).ap()
    Db = sbt("Db", [128, NH], F32).ap()
    arow = sbt("arow", [1, NH * Q], F32).ap()
    prow = sbt("prow", [1, NH], F32).ap()
    gbst = sbt("gbst", [128, H + NH], F32).ap()
    c_if = sbt("sb_c_if", [128, 128], F32).ap()
    c_ib = sbt("sb_c_ib", [128, 128], BF16).ap()
    c_orf = sbt("sb_c_orf", [1, 128], F32).ap()
    c_um = sbt("sb_c_um", [128, 128], F32).ap()
    convw_sb = sbt("sb_convw", [128, NPT_C * 4], F32).ap()

    PS = [nc.alloc_psum_tensor(f"ps{i}", [128, 512], F32) for i in range(8)]
    PSb4 = PS[4].ap().bitcast(BF16)

    with nc.Block() as block:
        @block.gpsimd
        def _(g):
            for dst, src in ((c_if, cst["c_if"]), (c_ib, cst["c_ib"]),
                             (c_orf, cst["c_orf"]), (c_um, cst["c_um"]),
                             (convw_sb, convw), (dtsb, dt_in)):
                C.dma(C.w, dst, src[:, :], "c")
            C.dma(C.w, acol[:, 1:2], dtb_in[:, :], "c")
            C.dma(C.w, acol[:, 0:1], alog[:, :], "c")
            t_c = C.dma(C.w, prow[0:1, 0:NH], Drow[:, :], "c")

            C.wtok(C.t, t_c)
            tp = C.inc(nc.tensor.matmul(PS[4].ap()[:, 0:NH], c_orf,
                                        prow[0:1, 0:NH], start=True,
                                        stop=True), "sP")
            C.wtok(C.v, tp)
            tv_db = C.inc(C.v.tensor_copy(Db, PS[4].ap()[:, 0:NH]), "sV")

            C.wtok(C.v, t_c)
            C.v.memset(zeros, 0.0)
            tv = C.inc(C.v.tensor_scalar(htmp, dtsb, acol[:, 1:2], None,
                                         ALU.add), "sV")
            C.wtok(C.s, tv)
            C.s.activation(sp_, htmp, AF.Abs)
            C.s.activation(sp_, sp_, AF.Exp, scale=-1.0)
            ta = C.inc(C.s.activation(sp_, sp_, AF.Ln, bias=1.0), "sA")
            C.wtok(C.v, ta)
            C.v.tensor_scalar(htmp, htmp, 0.0, None, ALU.max)
            tv = C.inc(C.v.tensor_tensor(sp_, sp_, htmp, ALU.add), "sV")
            C.wtok(C.s, tv)
            ta = C.inc(C.s.activation(acol[:, 0:1], acol[:, 0:1], AF.Exp),
                       "sA")
            C.wtok(C.v, ta)
            C.v.tensor_scalar(htmp, sp_, acol[:, 0:1], -1.0, ALU.mult,
                              ALU.mult)
            C.v.tensor_tensor_scan(gcum, htmp, zeros, 0.0, ALU.add, ALU.add)
            for c in range(NCH):
                gp = zeros[:, 0:1] if c == 0 else gcum[:, c * Q - 1:c * Q]
                C.v.tensor_scalar(arel[:, c * Q:(c + 1) * Q],
                                  gcum[:, c * Q:(c + 1) * Q], gp, None,
                                  ALU.subtract)
            for c in range(NCH):
                ae = arel[:, c * Q + Q - 1:c * Q + Q]
                C.v.tensor_scalar(w3[:, c * Q:(c + 1) * Q],
                                  arel[:, c * Q:(c + 1) * Q], ae, -1.0,
                                  ALU.subtract, ALU.mult)
            tv = C.inc(C.v.engine_nop(), "sV")
            C.wtok(C.s, tv)
            C.s.activation(erow, arel, AF.Exp)
            C.s.activation(eg, gcum, AF.Exp)
            ta = C.inc(C.s.activation(w3, w3, AF.Exp), "sA")
            C.wtok(C.v, ta)
            C.inc(C.v.tensor_tensor(w3, w3, sp_, ALU.mult), "sV")
            C.v.memset(SR, 0.0)
            sr_tok = C.inc(C.v.memset(SRb, 0.0), "sV")

            if _UPTO == "dtprep":
                _final_wait(C)
                return

            for c in range(NCH):
                C.wtok(C.w, C.latest("sV"), C.latest("sP"))
                C.wtok(C.t, C.latest("sV"), C.latest("sA"))
                C.wtok(C.s, C.latest("sP"))
                C.wtok(C.v, C.latest("ys"))

                t_x = C.dma(C.w, xin.rearrange("p (a b) -> p a b", b=131),
                            xbc[:, :, c * Q:c * Q + 131].transpose([1, 0, 2]),
                            "lx")
                t_ar = C.dma(C.w, arow_d[:, :], arel[:, c * Q:(c + 1) * Q],
                             "ar")
                C.wtok(C.w, t_ar)
                t_ar2 = C.dma(C.w, arow,
                              arow_d.ap().rearrange("a b -> (a b)")[None, :],
                              "ar")

                for src, slot in ((sp_, 0), (arel, 1), (erow, 2), (eg, 3),
                                  (w3, 5)):
                    i = nc.tensor.transpose(
                        PS[6].ap()[:, slot * 64:slot * 64 + 64],
                        src[:, c * Q:(c + 1) * Q], c_if[0:64, 0:64])
                    ttok = C.inc(i, "sP")
                i = nc.tensor.transpose(PS[6].ap()[0:1, 7 * 64:8 * 64],
                                        erow[:, c * Q + Q - 1:c * Q + Q],
                                        c_if[0:64, 0:64])
                ttok = C.inc(i, "sP")
                C.wtok(C.v, ttok)
                C.v.tensor_scalar(tcol[:, 64:128], PS[6].ap()[:, 64:128],
                                  -1.0, None, ALU.mult)
                for slot in (0, 2, 3, 5):
                    C.v.tensor_copy(tcol[:, slot * 64:(slot + 1) * 64],
                                    PS[6].ap()[:, slot * 64:(slot + 1) * 64])
                tv = C.inc(C.v.tensor_copy(prow[0:1, 0:NH],
                                           PS[6].ap()[0:1, 7 * 64:8 * 64]),
                           "sV")
                C.wtok(C.t, tv)
                i = nc.tensor.matmul(PS[6].ap()[:, 6 * 64:7 * 64], c_orf,
                                     prow[0:1, 0:NH], start=True, stop=True)
                tp = C.inc(i, "sP")
                C.wtok(C.v, tp)
                tv_tc = C.inc(C.v.tensor_copy(tcol[:, 6 * 64:7 * 64],
                                              PS[6].ap()[:, 6 * 64:7 * 64]),
                              "sV")
                C.wtok(C.w, tv_tc)
                C.dma(C.w, EGo[c], tcol[:, 3 * 64:4 * 64], "eg")

                C.wtok(C.v, t_x)
                cv3 = cv.rearrange("p (a b) -> p a b", b=128)
                x3 = xin.rearrange("p (a b) -> p a b", b=131)
                cw3 = convw_sb.rearrange("p (a b) -> p a b", b=4)
                for a0, a1 in ((0, 17), (17, 34)):
                    n = a1 - a0
                    tmp3 = yT.rearrange("p (a b) -> p a b", b=128)[:, 0:n, :]
                    cwb = lambda k: cw3[:, a0:a1, k:k + 1].broadcast_to(
                        [128, n, 128])
                    xvb = lambda k: x3[:, a0:a1, k:k + 128]
                    cvb = cv3[:, a0:a1, :]
                    C.v.tensor_tensor(cvb, xvb(0), cwb(0), ALU.mult)
                    for k in (1, 2, 3):
                        C.v.tensor_tensor(tmp3, xvb(k), cwb(k), ALU.mult)
                        C.v.tensor_tensor(cvb, cvb, tmp3, ALU.add)
                cvtok = C.inc(C.v.engine_nop(), "sV")
                C.wtok(C.s, cvtok)
                C.s.activation(xs, cv[:, 0:32 * 128], AF.Silu)
                C.s.activation(Bm, cv[:, 32 * 128:33 * 128], AF.Silu)
                ta_si = C.inc(C.s.activation(CMall[:, c * Q:(c + 1) * Q],
                                             cv[:, 33 * 128:34 * 128],
                                             AF.Silu), "sA")

                C.wtok(C.t, ta_si, tv_db)
                i = nc.tensor.matmul(PS[5].ap()[:, 0:128], Bm,
                                     CMall[:, c * Q:(c + 1) * Q],
                                     start=True, stop=True)
                tp = C.inc(i, "sP")
                C.wtok(C.v, tp)
                tvg = C.inc(C.v.tensor_tensor(G2m, PS[5].ap()[:, 0:128], c_um,
                                              ALU.mult), "sV")
                C.wtok(C.t, tvg)
                i = nc.tensor.transpose(PSb4[:, 0:128], Bm, c_ib)
                tp = C.inc(i, "sP")
                C.wtok(C.v, tp)
                tvb = C.inc(C.v.tensor_copy(BmT, PSb4[:, 0:128]), "sV")

                xstoks = []
                for p in range(32):
                    bkp = 6 + (p % 2)
                    if p >= 2:
                        C.wtok(C.t, xstoks[p - 2])
                    i = nc.tensor.transpose(PS[bkp].ap()[:, 0:128],
                                            xs[:, p * 128:(p + 1) * 128],
                                            c_if)
                    tp = C.inc(i, "sP")
                    C.wtok(C.v, tp)
                    xstoks.append(C.inc(C.v.tensor_copy(
                        xsT[:, p * 128:(p + 1) * 128],
                        PS[bkp].ap()[:, 0:128]), "sV"))

                for r in range(NH):
                    C.v.tensor_scalar(X3[:, r * 64:(r + 1) * 64],
                                      xsT[:, r * 64:(r + 1) * 64],
                                      tcol[:, 5 * 64 + r:5 * 64 + r + 1],
                                      None, ALU.mult)
                tv_x3 = C.inc(C.v.engine_nop(), "sV")

                C.wtok(C.t, t_ar2, tvb)
                t1_tok = {}
                e_tok = {}
                wr_tok = {}
                y_ev = {}

                arg_tok = {}

                def emit_T1(r):
                    qq = r % 4
                    bkq = 4 + qq // 2
                    off = (qq % 2) * 128
                    if r >= 4:
                        C.wtok(C.t, arg_tok[r - 4])
                    i = nc.tensor.matmul(PS[bkq].ap()[:, off:off + 128],
                                         c_orf, arow[0:1, r * Q:(r + 1) * Q],
                                         start=True, stop=True)
                    t1_tok[r] = C.inc(i, "sP")

                def emit_EW(r):
                    qq = r % 4
                    bkq = 4 + qq // 2
                    off = (qq % 2) * 128
                    C.wtok(C.v, t1_tok[r])
                    if r >= 4:
                        C.wtok(C.v, wr_tok[r - 4])
                    arg_tok[r] = C.inc(C.v.tensor_scalar(
                        Et[:, qq * 128:(qq + 1) * 128],
                        PS[bkq].ap()[:, off:off + 128],
                        tcol[:, 64 + r:64 + r + 1], 0.0,
                        ALU.add, ALU.min), "sV")
                    C.wtok(C.s, arg_tok[r])
                    e_tok[r] = C.inc(C.s.activation(
                        Et[:, qq * 128:(qq + 1) * 128],
                        Et[:, qq * 128:(qq + 1) * 128], AF.Exp), "sA")
                    C.wtok(C.v, e_tok[r])
                    wr_tok[r] = C.inc(C.v.scalar_tensor_tensor(
                        wr[:, qq * 128:(qq + 1) * 128],
                        Et[:, qq * 128:(qq + 1) * 128],
                        tcol[:, r:r + 1], G2m, ALU.mult, ALU.mult), "sV")

                def emit_intra(r):
                    qq = r % 4
                    gidx = r // 8
                    hh = r % 8
                    C.wtok(C.t, wr_tok[r])
                    if hh == 0 and gidx >= 2:
                        C.wtok(C.t, y_ev[gidx - 2])
                    i = nc.tensor.matmul(
                        PS[gidx % 2].ap()[:, hh * 64:(hh + 1) * 64],
                        wr[:, qq * 128:(qq + 1) * 128],
                        xsT[:, r * 64:(r + 1) * 64], start=True, stop=True)
                    C.inc(i, "sP")
                    if hh == 7:
                        if gidx >= 2:
                            C.wtok(C.t, y_ev[gidx - 2])
                        i = nc.tensor.matmul(
                            PS[2 + gidx % 2].ap()[:, :],
                            CMall[:, c * Q:(c + 1) * Q],
                            SRb[:, gidx * 512:(gidx + 1) * 512],
                            start=True, stop=True)
                        tp_g = C.inc(i, "sP")
                        C.wtok(C.v, tp_g)
                        for h2 in range(8):
                            rr = gidx * 8 + h2
                            C.v.tensor_scalar(
                                Et[:, 512:576],
                                PS[2 + gidx % 2].ap()[:, h2 * 64:(h2 + 1) * 64],
                                tcol[:, 2 * 64 + rr:2 * 64 + rr + 1], None,
                                ALU.mult)
                            C.v.tensor_tensor(
                                Et[:, 576:640],
                                PS[gidx % 2].ap()[:, h2 * 64:(h2 + 1) * 64],
                                Et[:, 512:576], ALU.add)
                            C.v.scalar_tensor_tensor(
                                yT[:, rr * 64:(rr + 1) * 64],
                                xsT[:, rr * 64:(rr + 1) * 64],
                                Db[:, rr:rr + 1], Et[:, 576:640],
                                ALU.mult, ALU.add)
                        y_ev[gidx] = C.inc(C.v.engine_nop(), "sV")

                emit_T1(0)
                emit_EW(0)
                emit_T1(1)
                emit_EW(1)
                emit_T1(2)
                emit_EW(2)
                for r in range(NH):
                    if r + 3 < NH:
                        emit_T1(r + 3)
                        emit_EW(r + 3)
                    emit_intra(r)

                C.wtok(C.w, y_ev[7])
                C.dma(C.w, Yo[c], yT, "ys")

                C.wtok(C.t, tv_x3, y_ev[7])
                st_ev = {}
                for gi in range(8):
                    if gi >= 2:
                        C.wtok(C.t, st_ev[gi - 2])
                    i = nc.tensor.matmul(PS[2 + gi % 2].ap()[:, :], BmT,
                                         X3[:, gi * 512:(gi + 1) * 512],
                                         start=True, stop=True)
                    tp = C.inc(i, "sP")
                    C.wtok(C.v, tp)
                    if gi == 0:
                        C.wtok(C.v, sr_tok)
                    for h2 in range(8):
                        rr = gi * 8 + h2
                        C.v.scalar_tensor_tensor(
                            SR[:, rr * 64:(rr + 1) * 64],
                            SR[:, rr * 64:(rr + 1) * 64],
                            tcol[:, 6 * 64 + rr:6 * 64 + rr + 1],
                            PS[2 + gi % 2].ap()[:, h2 * 64:(h2 + 1) * 64],
                            ALU.mult, ALU.add)
                    st_ev[gi] = C.inc(C.v.engine_nop(), "sV")
                sr_tok = C.inc(C.v.tensor_copy(SRb, SR), "sV")

            C.wtok(C.w, C.latest("sA"))
            C.dma(C.w, CMo[:, :], CMall, "os")
            C.wtok(C.v, sr_tok)
            C.inc(C.v.tensor_copy(gbst[:, 0:H], SR), "sV")
            C.wtok(C.t, C.latest("sV"))
            i = nc.tensor.transpose(PS[6].ap()[0:1, 0:NH],
                                    gcum[:, TPC - 1:TPC], c_if[0:64, 0:64])
            tp = C.inc(i, "sP")
            C.wtok(C.v, tp)
            tv = C.inc(C.v.tensor_copy(prow[0:1, 0:NH],
                                       PS[6].ap()[0:1, 0:NH]), "sV")
            C.wtok(C.t, tv)
            i = nc.tensor.matmul(PS[6].ap()[:, 64:128], c_orf,
                                 prow[0:1, 0:NH], start=True, stop=True)
            tp = C.inc(i, "sP")
            C.wtok(C.v, tp)
            tv = C.inc(C.v.tensor_copy(gbst[:, H:H + NH],
                                       PS[6].ap()[:, 64:128]), "sV")
            C.wtok(C.w, tv)
            C.dma(C.w, GBo[:, :], gbst, "os")
            _final_wait(C)
    return nc



def _build_L3():
    nc = bass.Bass("TRN2", target_bir_lowering=False, detect_race_conditions=_DETECT)
    GB8 = nc.dram_tensor("GB8", [8, 128, H + NH], F32, kind="ExternalInput")
    Yi = nc.dram_tensor("Yi", [NCH, 128, H], F32, kind="ExternalInput")
    CMi = nc.dram_tensor("CMi", [128, TPC], BF16, kind="ExternalInput")
    EGi = nc.dram_tensor("EGi", [NCH, 128, NH], F32, kind="ExternalInput")
    Zi = nc.dram_tensor("Zi", [32, 128, TPC], F32, kind="ExternalInput")
    R1i = nc.dram_tensor("R1i", [32, 128, TPC], F32, kind="ExternalInput")
    smask = nc.dram_tensor("smask", [1, 8], F32, kind="ExternalInput")
    mbig = nc.dram_tensor("mbig", [1, 8], F32, kind="ExternalInput")
    selv = nc.dram_tensor("selv", [1, 8], F32, kind="ExternalInput")
    Wop = nc.dram_tensor("Wop", [32, 128, NKT, 128], F32, kind="ExternalInput")
    nw = nc.dram_tensor("nw", [128, 32], F32, kind="ExternalInput")
    Wg = nc.dram_tensor("Wg", [NMI_G, 128, NKT, 128], I32, kind="ExternalInput")
    Wu = nc.dram_tensor("Wu", [NMI_G, 128, NKT, 128], I32, kind="ExternalInput")
    Wd = nc.dram_tensor("Wd", [32, 128, NMI_G, 128], I32, kind="ExternalInput")
    sgate = nc.dram_tensor("sgate", [128, NMI_G], F32, kind="ExternalInput")
    sup = nc.dram_tensor("sup", [128, NMI_G], F32, kind="ExternalInput")
    sdown = nc.dram_tensor("sdown", [128, 32], F32, kind="ExternalInput")
    cs = _np_consts()
    c_if_d = nc.inline_tensor(cs["c_if"], name="c_if")
    c_ocb_d = nc.inline_tensor(cs["c_ocb"], name="c_ocb")
    c_orf_d = nc.inline_tensor(cs["c_orf"], name="c_orf")

    out = nc.dram_tensor("out", [TPC, H], F32, kind="ExternalOutput")
    R2 = nc.dram_tensor("R2", [32, 128, TPC], F32)
    h2d = nc.dram_tensor("h2d", [NMI_G, 128, 512], BF16)
    wdbf = nc.dram_tensor("wdbf", [32, 128, INTER], BF16)

    C = Ctx(nc)
    _alloc_common(C, nc, final_path=True)
    C.sb_sgate = nc.alloc_sbuf_tensor("sb_sgate", [128, NMI_G], F32).ap()
    C.sb_sup = nc.alloc_sbuf_tensor("sb_sup", [128, NMI_G], F32).ap()
    C.sb_sdown = nc.alloc_sbuf_tensor("sb_sdown", [128, 32], F32).ap()
    C.sb_nw = nc.alloc_sbuf_tensor("sb_nw", [128, 32], F32).ap()
    C.c_if = nc.alloc_sbuf_tensor("sb_c_if", [128, 128], F32).ap()
    C.c_ocb = nc.alloc_sbuf_tensor("sb_c_ocb", [128, 1], BF16).ap()
    C.c_orf = nc.alloc_sbuf_tensor("sb_c_orf", [1, 128], F32).ap()
    yt = C.Bf32[:, 0:4096]
    yf = C.Bf32[:, 4096:8192]
    zc = C.Bf32[:, 8192:12288]
    gbuf = C.Bf32[:, 12288:16384]
    SIn = C.Bf32[:, 16384:20480]
    SIb = C.wut[0][:, :]
    EGsb = nc.alloc_sbuf_tensor("EGsb", [128, NCH * NH], F32).ap()
    CMsb = nc.alloc_sbuf_tensor("CMsb", [128, TPC], BF16).ap()
    Gc = nc.alloc_sbuf_tensor("Gc", [128, 8 * NH], F32).ap()
    gsel = nc.alloc_sbuf_tensor("gsel", [128, NH], F32).ap()
    coef = nc.alloc_sbuf_tensor("coef", [128, NH], F32).ap()
    mrows = nc.alloc_sbuf_tensor("mrows", [1, 24], F32).ap()
    mbc = nc.alloc_sbuf_tensor("mbc", [128, 24], F32).ap()

    PS = [nc.alloc_psum_tensor(f"ps{i}", [128, 512], F32) for i in range(8)]

    with nc.Block() as block:
        @block.gpsimd
        def _(g):
            for dst, src in ((C.sb_sgate, sgate), (C.sb_sup, sup),
                             (C.sb_sdown, sdown), (C.sb_nw, nw),
                             (C.c_if, c_if_d), (C.c_ocb, c_ocb_d),
                             (C.c_orf, c_orf_d), (CMsb, CMi)):
                C.dma(C.w, dst, src[:, :], "c")
            C.dma(C.w, mrows[0:1, 0:8], smask[:, :], "c")
            C.dma(C.w, mrows[0:1, 8:16], mbig[:, :], "c")
            C.dma(C.w, mrows[0:1, 16:24], selv[:, :], "c")
            t_eg = C.dma(C.w, EGsb.rearrange("p (a b) -> p a b", b=NH),
                         EGi[:, :, :].transpose([1, 0, 2]), "c")
            t_c = C.latest("c")

            C.wtok(C.t, t_c)
            tp = C.inc(nc.tensor.matmul(PS[4].ap()[:, 0:24], C.c_orf,
                                        mrows[0:1, 0:24], start=True,
                                        stop=True), "sP")
            C.wtok(C.v, tp)
            C.inc(C.v.tensor_copy(mbc, PS[4].ap()[:, 0:24]), "sV")

            lp = None
            for j in range(8):
                lp = C.dma(C.w, Gc[:, j * NH:(j + 1) * NH],
                           GB8[j, :, H:H + NH], "c")
            C.wtok(C.v, lp)
            for j in range(1, 8):
                C.v.tensor_tensor(Gc[:, j * NH:(j + 1) * NH],
                                  Gc[:, (j - 1) * NH:j * NH],
                                  Gc[:, j * NH:(j + 1) * NH], ALU.add)
            C.v.tensor_scalar(gsel, Gc[:, 0:NH], mbc[:, 16:17], None,
                              ALU.mult)
            for j in range(1, 8):
                C.v.scalar_tensor_tensor(gsel, Gc[:, j * NH:(j + 1) * NH],
                                         mbc[:, 16 + j:17 + j], gsel,
                                         ALU.mult, ALU.add)
            C.v.memset(SIn, 0.0)
            sj_tok = C.inc(C.v.engine_nop(), "sV")
            for j in range(8):
                C.v.tensor_tensor(coef, gsel, Gc[:, j * NH:(j + 1) * NH],
                                  ALU.subtract)
                tv = C.inc(C.v.tensor_scalar(coef, coef, mbc[:, j:j + 1],
                                             mbc[:, 8 + j:9 + j],
                                             ALU.mult, ALU.add), "sV")
                C.wtok(C.s, tv)
                ta = C.inc(C.s.activation(coef, coef, AF.Exp), "sA")
                C.wtok(C.w, sj_tok)
                t_sj = C.dma(C.w, yt, GB8[j, :, 0:H], "ly")
                C.wtok(C.v, ta, t_sj)
                for r in range(NH):
                    C.v.scalar_tensor_tensor(SIn[:, r * 64:(r + 1) * 64],
                                             yt[:, r * 64:(r + 1) * 64],
                                             coef[:, r:r + 1],
                                             SIn[:, r * 64:(r + 1) * 64],
                                             ALU.mult, ALU.add)
                sj_tok = C.inc(C.v.engine_nop(), "sV")
            sib_tok = C.inc(C.v.tensor_copy(SIb, SIn), "sV")
            if _UPTO == "sin":
                _final_wait(C)
                return

            C.wtok(C.v, t_eg)
            xo_tok = None
            for cch in range(NCH):
                C.wtok(C.w, C.latest("sV"), C.latest("sP"))
                C.wtok(C.t, C.latest("sV"))
                t_y = C.dma(C.w, yt, Yi[cch], "ly")
                t_z = C.dma(C.w, zc.rearrange("p (a b) -> p a b", b=128),
                            Zi[:, :, cch * Q:(cch + 1) * Q].transpose([1, 0, 2]),
                            "lz")
                cor = {}
                for gi in range(8):
                    if gi >= 2:
                        C.wtok(C.t, cor[gi - 2])
                    if gi < 2:
                        C.wtok(C.t, sib_tok)
                    i = nc.tensor.matmul(PS[gi % 2].ap()[:, :],
                                         CMsb[:, cch * Q:(cch + 1) * Q],
                                         SIb[:, gi * 512:(gi + 1) * 512],
                                         start=True, stop=True)
                    tp = C.inc(i, "sP")
                    C.wtok(C.v, tp)
                    if gi == 0:
                        C.wtok(C.v, t_y)
                    for h2 in range(8):
                        rr = gi * 8 + h2
                        C.v.scalar_tensor_tensor(
                            yt[:, rr * 64:(rr + 1) * 64],
                            PS[gi % 2].ap()[:, h2 * 64:(h2 + 1) * 64],
                            EGsb[:, cch * NH + rr:cch * NH + rr + 1],
                            yt[:, rr * 64:(rr + 1) * 64], ALU.mult, ALU.add)
                    cor[gi] = C.inc(C.v.engine_nop(), "sV")
                ytoks = []
                for p in range(32):
                    bkp = 2 + (p % 2)
                    if p >= 2:
                        C.wtok(C.t, ytoks[p - 2])
                    if p == 0:
                        C.wtok(C.t, cor[7])
                    i = nc.tensor.transpose(PS[bkp].ap()[:, 0:128],
                                            yt[:, p * 128:(p + 1) * 128],
                                            C.c_if)
                    tp = C.inc(i, "sP")
                    C.wtok(C.v, tp)
                    ytoks.append(C.inc(C.v.tensor_copy(
                        yf[:, p * 128:(p + 1) * 128],
                        PS[bkp].ap()[:, 0:128]), "sV"))
                C.wtok(C.s, ytoks[-1], t_z)
                ta = C.inc(C.s.activation(zc, zc, AF.Silu), "sA")
                C.wtok(C.v, ta)
                C.inc(C.v.tensor_tensor(gbuf, yf, zc, ALU.mult), "sV")
                ptoks = []
                for pt in range(32):
                    b = (pt % 2) * 1024
                    if pt >= 2:
                        C.wtok(C.v, ptoks[pt - 2])
                    C.v.tensor_copy(C.xb2[:, b:b + 128],
                                    gbuf[:, pt * 128:(pt + 1) * 128])
                    tv = C.inc(C.v.engine_nop(), "sV")
                    C.wtok(C.s, tv)
                    ts = C.inc(C.s.activation(C.xb2[:, b + 512:b + 640],
                                              C.xb2[:, b:b + 128],
                                              AF.Square), "sA")
                    C.wtok(C.t, ts)
                    i = nc.tensor.matmul(PS[6].ap()[0:1, 0:128], C.c_ocb,
                                         C.xb2[:, b + 512:b + 640],
                                         start=(pt == 0), stop=(pt == 31))
                    ptoks.append(C.inc(i, "sP"))
                rowa = C.row_m[0:1, 0:128]
                C.wtok(C.v, ptoks[-1])
                tv = C.inc(C.v.tensor_scalar(rowa, PS[6].ap()[0:1, 0:128],
                                             1.0 / H, EPS, ALU.mult, ALU.add),
                           "sV")
                C.wtok(C.s, tv)
                C.s.activation(rowa, rowa, AF.Ln)
                ta = C.inc(C.s.activation(rowa, rowa, AF.Exp, scale=-0.5),
                           "sA")
                C.wtok(C.t, ta)
                tp = C.inc(nc.tensor.matmul(PS[7].ap()[:, 0:128], C.c_orf,
                                            rowa, start=True, stop=True),
                           "sP")
                C.wtok(C.v, tp)
                g3 = gbuf.rearrange("p (a b) -> p a b", b=128)
                rbc = PS[7].ap()[:, 0:128].unsqueeze(1).broadcast_to(
                    [128, 32, 128])
                nbc = C.sb_nw.unsqueeze(2).broadcast_to([128, 32, 128])
                a3 = C.A.ap().rearrange("p (a b) -> p a b", b=1024)[
                    :, :, cch * 128:cch * 128 + 128]
                C.v.tensor_tensor(yf.rearrange("p (a b) -> p a b", b=128),
                                  g3, rbc, ALU.mult)
                xo_tok = C.inc(C.v.tensor_tensor(
                    a3, yf.rearrange("p (a b) -> p a b", b=128), nbc,
                    ALU.mult), "sV")
            if _UPTO == "m4":
                _final_wait(C)
                return

            wk = {}
            evo = {}
            last_store = None
            for mi in range(32 + 1):
                if mi < 32:
                    if mi < 2:
                        C.wtok(C.g, C.latest("sP"))
                    else:
                        C.wtok(C.g, evo[(mi - 2, "pe")])
                    wk[mi] = C.dma(C.g, C.wgt[mi % 2][:, :],
                                   Wop[mi].rearrange("p a b -> p (a b)"),
                                   f"wg{mi % 2}")
                if mi >= 1:
                    j = mi - 1
                    C.wtok(C.t, wk[j])
                    if j < 2:
                        C.wtok(C.t, xo_tok)
                    else:
                        C.wtok(C.t, evo[(j - 2, "dve")])
                    b0 = 4 + (j % 2) * 2
                    for kt in range(NKT):
                        lw = C.wgt[j % 2][:, kt * 128:(kt + 1) * 128]
                        st = (kt == 0)
                        sp = (kt == NKT - 1)
                        nc.tensor.matmul(PS[b0][:, :], lw,
                                         C.A[:, kt * 1024:kt * 1024 + 512],
                                         start=st, stop=sp)
                        i = nc.tensor.matmul(PS[b0 + 1][:, :], lw,
                                             C.A[:, kt * 1024 + 512:(kt + 1) * 1024],
                                             start=st, stop=sp)
                    evo[(j, "pe")] = C.inc(i, "sP")
                    tv = None
                    for hf in (0, 1):
                        t_rx = C.dma(C.w, C.rx,
                                     R1i[j, :, hf * 512:(hf + 1) * 512], "rx")
                        C.wtok(C.v, evo[(j, "pe")], t_rx, last_store)
                        tv = C.inc(C.v.tensor_tensor(C.ost, PS[b0 + hf][:, :],
                                                     C.rx, ALU.add), "sV")
                        C.wtok(C.w, tv)
                        last_store = C.dma(C.w,
                                           R2[j, :, hf * 512:(hf + 1) * 512],
                                           C.ost, "os")
                    evo[(j, "dve")] = tv
            if _UPTO == "outproj":
                _final_wait(C)
                return

            for hf in (0, 1):
                nrm_tok = _emit_norm_half(C, R2, hf, "ln", PS)
            _emit_ffn(C, R2, None, PS, Wg, Wu, Wd, wdbf, h2d, final_out=out,
                      a_ready=nrm_tok)
            _final_wait(C)
    return nc



def kernel(**inputs):
    _ = inputs.get("L0_attn_residual")
    hp = _prep_host(inputs)

    shared1 = {k: hp[k] for k in
               ("Wg", "Wu", "Wd", "Wip", "sgate", "sup", "sdown", "mup",
                "lnw")}
    nc1 = _build_L1()
    maps1 = [dict(shared1, x_in=hp["xs"][c]) for c in range(8)]
    res1 = run_bass_kernel_spmd(nc1, maps1, core_ids=list(range(8)))
    r1 = res1.results

    nc2 = _build_L2()
    maps2 = []
    for c in range(8):
        xbc_s = np.zeros((NPT_C, 128, TPC + 3), np.float32)
        xbc_s[:, :, 3:] = r1[c]["XBCo"]
        if c % 4 != 0:
            xbc_s[:, :, 0:3] = r1[c - 1]["XBCo"][:, :, TPC - 3:TPC]
        maps2.append({
            "xbc": xbc_s, "dt_in": r1[c]["DTo"], "alog": hp["alog"],
            "dtb_in": hp["dtb"], "Drow": hp["Drow"], "convw": hp["convw"],
        })
    res2 = run_bass_kernel_spmd(nc2, maps2, core_ids=list(range(8)))
    r2 = res2.results

    GB8 = np.ascontiguousarray(np.stack([r2[c]["GBo"] for c in range(8)]))
    nc3 = _build_L3()
    shared3 = {k: hp[k] for k in ("Wg", "Wu", "Wd", "Wop", "sgate", "sup",
                                  "sdown", "nw")}
    maps3 = []
    for c in range(8):
        sm = np.zeros((1, 8), np.float32)
        sv = np.zeros((1, 8), np.float32)
        for j in range(8):
            if j < c and j // 4 == c // 4:
                sm[0, j] = 1.0
        if c % 4 != 0:
            sv[0, c - 1] = 1.0
        mb = (-30000.0) * (1.0 - sm)
        maps3.append(dict(
            shared3, GB8=GB8, Yi=r2[c]["Yo"], CMi=r2[c]["CMo"],
            EGi=r2[c]["EGo"], Zi=r1[c]["Zo"], R1i=r1[c]["R1o"],
            smask=sm, mbig=mb, selv=sv))
    res3 = run_bass_kernel_spmd(nc3, maps3, core_ids=list(range(8)))
    outs = [res3.results[c]["out"] for c in range(8)]
    return np.concatenate(outs, axis=0).reshape(2, 4096, H).astype(np.float32)
